# version 50
# baseline (speedup 1.0000x reference)
import numpy as np
import contextlib
import concourse.bass as bass
import concourse.mybir as mybir
from concourse.bass_utils import run_bass_kernel_spmd

F32 = mybir.dt.float32
BF16 = mybir.dt.bfloat16
AF = mybir.ActivationFunctionType
ALU = mybir.AluOpType
AX = mybir.AxisListType.X

D = 1024
DC = 8
HID = 4096
HC = 32
NP = 8
QG = 512
GB = 512
EPS = 1e-5
NCORES = 8

STREAM_OF = {"t": "t", "s": "s", "v": "v", "g": "g", "y": "y",
             "dy": "y", "dg": "g", "ds": "s"}


class KB:

    def __init__(self, nc):
        self.nc = nc
        self.h = {}
        self.n = {}
        self.waited = {}
        self.bufs = {}
        self.prog = {k: [] for k in "tsvgy"}

    def mksem(self, ctx, *keys):
        self.ctx = ctx
        for k in keys:
            self.h[k] = ctx.enter_context(self.nc.semaphore(k))
            self.n[k] = 0

    def getsem(self, k):
        if k not in self.h:
            self.h[k] = self.ctx.enter_context(self.nc.semaphore(k))
            self.n[k] = 0
        return self.h[k]

    def settle(self, bufnames, key):
        for b in bufnames:
            self.bufs[b] = [(key, self.n[key]), []]

    def dma(self, qkey, semkey, maker, reads=(), writes=()):
        self.getsem(semkey)
        return self.plan(semkey, maker, reads, writes, inc=16, stream=qkey)

    def plan(self, key, makers, reads=(), writes=(), inc=1, stream=None):
        self.getsem(key)
        stream = stream or STREAM_OF[key]
        waits = {}
        for b in reads:
            s = self.bufs.get(b)
            if s and s[0]:
                k2, v2 = s[0]
                waits[k2] = max(waits.get(k2, 0), v2)
        for b in writes:
            s = self.bufs.get(b)
            if s:
                if s[0]:
                    k2, v2 = s[0]
                    waits[k2] = max(waits.get(k2, 0), v2)
                for (k2, v2) in s[1]:
                    waits[k2] = max(waits.get(k2, 0), v2)
        w = self.waited.setdefault(stream, {})
        fw = []
        for k2, v2 in waits.items():
            if v2 > 0 and w.get(k2, 0) < v2:
                fw.append((self.h[k2], v2))
                w[k2] = v2
        self.n[key] += inc
        cnt = self.n[key]
        for b in reads:
            self.bufs.setdefault(b, [None, []])[1].append((key, cnt))
        for b in writes:
            self.bufs[b] = [(key, cnt), []]
        sem = self.h[key]
        if not isinstance(makers, (list, tuple)):
            makers = [makers]

        def go(eng, fw=fw, makers=makers, sem=sem, inc=inc):
            for (hh, vv) in fw:
                eng.wait_ge(hh, vv)
            ins = None
            for m in makers:
                ins = m(eng)
            ins.then_inc(sem, inc)
        self.prog[stream].append(go)
        return cnt

    def barrier(self, stream, keys=None):
        keys = list(self.h.keys()) if keys is None else keys
        w = self.waited.setdefault(stream, {})
        fw = []
        for k in keys:
            if self.n[k] > 0 and w.get(k, 0) < self.n[k]:
                fw.append((self.h[k], self.n[k]))
                w[k] = self.n[k]

        def go(eng, fw=fw):
            for (hh, vv) in fw:
                eng.wait_ge(hh, vv)
        self.prog[stream].append(go)

    def final_wait(self, stream, key):
        sem, val = self.h[key], self.n[key]

        def go(eng):
            eng.wait_ge(sem, val)
        self.prog[stream].append(go)


def build(S=4096, upto="full"):
    CT = S // 128
    NQG = S // QG
    NG = S // GB
    NB = (CT + 5) // 6

    nc = bass.Bass("TRN2", target_bir_lowering=False)
    dram = lambda n, sh, dt=F32, kind="ExternalInput": nc.dram_tensor(n, sh, dt, kind=kind).ap()

    xT = dram("xT", [D, S])
    wqT = dram("wqT", [D, D]); wkT = dram("wkT", [D, D])
    wvT = dram("wvT", [D, D]); woT = dram("woT", [D, D])
    bo_r = dram("bo_r", [128, DC])
    w1T = dram("w1T", [D, HID]); w2T = dram("w2T", [D, HID])
    w3T = dram("w3T", [HID, D])
    h1T = dram("h1T", [HID, HID]); h2T = dram("h2T", [HID, HID])
    w1s = dram("w1s", [128, HC]); w2s = dram("w2s", [128, HC])
    h1s = dram("h1s", [128, HC]); h2s = dram("h2s", [128, HC])
    w3s = dram("w3s", [128, DC])
    outT = dram("outT", [D, S], kind="ExternalOutput")
    w1B = dram("w1B", [HC, 128, D], BF16, kind="Internal")
    w2B = dram("w2B", [HC, 128, D], BF16, kind="Internal")
    w3B = dram("w3B", [DC, 128, HID], BF16, kind="Internal")
    h1B = dram("h1B", [HC, 128, HID], BF16, kind="Internal")
    h2B = dram("h2B", [HC, 128, HID], BF16, kind="Internal")
    woB = dram("woB", [DC, 128, D], BF16, kind="Internal")
    attnS = dram("attnS", [D, S], BF16, kind="ExternalOutput")

    ctx = contextlib.ExitStack()
    with ctx:
        kb = KB(nc)
        kb.mksem(ctx, "t", "s", "v", "g")
        sb = lambda n, sh, dt=F32: ctx.enter_context(nc.sbuf_tensor(n, sh, dt))

        ones_b = sb("ones_b", [128, 128], BF16)
        ones_f = sb("ones_f", [128, 128], F32)
        epsT = sb("epsT", [128, 1], F32)
        selA = sb("selA", [128, 1], BF16)
        selB = sb("selB", [128, 1], BF16)
        bo_sb = sb("bo_sb", [128, DC], F32)
        w1s_sb = sb("w1s_sb", [128, HC], F32)
        w2s_sb = sb("w2s_sb", [128, HC], F32)
        h1s_sb = sb("h1s_sb", [128, HC], F32)
        h2s_sb = sb("h2s_sb", [128, HC], F32)
        w3s_sb = sb("w3s_sb", [128, DC], F32)
        wosl = [sb(f"wosl{i}", [128, D], BF16) for i in range(2)]
        w1sl = [sb(f"w1sl{i}", [128, D], BF16) for i in range(3)]
        w2sl = [sb(f"w2sl{i}", [128, D], BF16) for i in range(3)]
        h1sl = [sb(f"h1sl{i}", [128, 2048], BF16) for i in range(4)]
        h2sl = [sb(f"h2sl{i}", [128, 2048], BF16) for i in range(4)]
        meanT = sb("meanT", [128, 512], F32)
        t1T = sb("t1T", [128, 512], F32)
        t2T = sb("t2T", [128, 512], F32)
        rbcT = sb("rbcT", [128, 512], F32)
        mrbcT = sb("mrbcT", [128, 512], F32)
        lt1 = sb("lt1", [128, 512], F32)
        lt1g = meanT
        sqT = sb("sqT", [128, 1024], BF16)

        psum = ctx.enter_context(nc.psum_tensor("psum", [128, 4096], F32))
        PB = lambda k: psum[:, 512 * k:512 * (k + 1)]

        kb.plan("v", lambda v: v.memset(ones_b[:], 1.0), writes=["ones_b"])
        kb.plan("v", lambda v: v.memset(ones_f[:], 1.0), writes=["ones_f"])
        kb.plan("v", lambda v: v.memset(epsT[:], EPS), writes=["epsT"])
        kb.plan("v", lambda v: v.memset(selA[:], 0.0), writes=["selA"])
        kb.plan("v", lambda v: v.memset(selA[0:64, :], 1.0), writes=["selA"])
        kb.plan("v", lambda v: v.memset(selB[:], 0.0), writes=["selB"])
        kb.plan("v", lambda v: v.memset(selB[64:128, :], 1.0), writes=["selB"])
        _scl = [(bo_sb, bo_r, "bo_sb"), (w1s_sb, w1s, "w1s_sb"),
                (w2s_sb, w2s, "w2s_sb"), (h1s_sb, h1s, "h1s_sb"),
                (h2s_sb, h2s, "h2s_sb"), (w3s_sb, w3s, "w3s_sb")]
        for (dst, src, nm) in _scl:
            kb.dma("y", "dsc", lambda y, d=dst, s=src: y.dma_start(out=d[:], in_=s),
                   writes=[nm])
        kb.settle([nm for (_, _, nm) in _scl], "dsc")

        actx = contextlib.ExitStack()
        xbf = [actx.enter_context(nc.sbuf_tensor(f"xbf{i}", [128, S], BF16)) for i in range(DC)]
        qPs = [actx.enter_context(nc.sbuf_tensor(f"qP{i}", [128, S], BF16)) for i in range(2)]
        kPs = [actx.enter_context(nc.sbuf_tensor(f"kP{i}", [128, S], BF16)) for i in range(2)]
        vPs = [actx.enter_context(nc.sbuf_tensor(f"vP{i}", [128, (S // 128) * 130], BF16))
               for i in range(2)]
        for i in range(2):
            kb.plan("v", lambda v, i=i: v.memset(
                vPs[i].rearrange("p (c k) -> p c k", k=130)[:, :, 64:65], 1.0),
                writes=[f"vP{i}"])
            kb.plan("v", lambda v, i=i: v.memset(
                vPs[i].rearrange("p (c k) -> p c k", k=130)[:, :, 129:130], 1.0),
                writes=[f"vP{i}"])
        pT = actx.enter_context(nc.sbuf_tensor("pT", [128, 6 * 1024], BF16))
        recT = actx.enter_context(nc.sbuf_tensor("recT", [128, 1024], BF16))
        rbc2 = actx.enter_context(nc.sbuf_tensor("rbc2", [128, 1024], F32))
        nmT = actx.enter_context(nc.sbuf_tensor("nmT", [128, 24], F32))
        negM = actx.enter_context(nc.sbuf_tensor("negM", [128, 4], F32))
        att_st = [actx.enter_context(nc.sbuf_tensor(f"att_st{i}", [128, 2 * QG], BF16))
                  for i in range(2)]

        for i in range(DC):
            kb.dma("g", f"Dgx{i}", lambda g, i=i: g.dma_start(
                out=xbf[i][:], in_=xT[i * 128:(i + 1) * 128, :]),
                writes=[f"xbf{i}"])

        p0_jobs = []
        for (srcT, dstB, rows, nm) in [(woT, woB, D, "woB"), (w1T, w1B, D, "w1B"),
                                       (w2T, w2B, D, "w2B"), (w3T, w3B, HID, "w3B"),
                                       (h1T, h1B, HID, "h1B"), (h2T, h2B, HID, "h2B")]:
            cols = srcT.shape[1]
            cw = min(2048, cols)
            for r in range(rows // 128):
                for cc in range(cols // cw):
                    p0_jobs.append((srcT, dstB, nm, r, cc, cw))
        p0_state = {"i": 0, "slot": 0}

        def p0_drain(k):
            end = min(len(p0_jobs), p0_state["i"] + k)
            while p0_state["i"] < end:
                (srcT, dstB, nm, r, cc, cw) = p0_jobs[p0_state["i"]]
                sl = p0_state["slot"] % 4
                p0_state["slot"] += 1
                p0_state["i"] += 1
                buf = h1sl[sl] if sl < 2 else h2sl[sl - 2]
                bname = f"h1sl{sl}" if sl < 2 else f"h2sl{sl - 2}"
                ns = cw // 128
                kb.dma("g", f"Dg_{bname}", lambda g, b=buf, s=srcT, r=r, cc=cc, cw=cw:
                       g.dma_start(out=b[:, 0:cw],
                                   in_=s[r * 128:(r + 1) * 128, cc * cw:(cc + 1) * cw]),
                       writes=[bname])
                kb.dma("y", f"Dy0_{bname}", lambda y, b=buf, d=dstB, r=r, cc=cc, cw=cw, ns=ns:
                       y.dma_start(
                           out=d.rearrange("s p f -> p s f")
                           [:, cc * ns:(cc + 1) * ns, r * 128:(r + 1) * 128],
                           in_=b[:, 0:cw].rearrange("p (s oc) -> p s oc", oc=128)),
                       reads=[bname])

        def ln_finalize(sum_b, sq_b, w, inv):
            kb.plan("v", lambda v: v.tensor_scalar(
                out=meanT[:, 0:w], in0=PB(sum_b)[:, 0:w], scalar1=inv,
                scalar2=None, op0=ALU.mult),
                reads=[f"pb{sum_b}"], writes=["meanT"])
            kb.plan("v", lambda v: v.tensor_tensor(
                out=t1T[:, 0:w], in0=meanT[:, 0:w], in1=meanT[:, 0:w], op=ALU.mult),
                reads=["meanT"], writes=["t1T"])
            kb.plan("v", lambda v: v.scalar_tensor_tensor(
                out=t2T[:, 0:w], in0=PB(sq_b)[:, 0:w], scalar=inv, in1=t1T[:, 0:w],
                op0=ALU.mult, op1=ALU.subtract),
                reads=[f"pb{sq_b}", "t1T"], writes=["t2T"])
            kb.plan("s", lambda s: s.activation(
                out=t1T[:, 0:w], in_=t2T[:, 0:w], func=AF.Ln, bias=epsT[:, 0:1]),
                reads=["t2T", "epsT"], writes=["t1T"])
            kb.plan("s", lambda s: s.activation(
                out=rbcT[:, 0:w], in_=t1T[:, 0:w], func=AF.Exp, scale=-0.5),
                reads=["t1T"], writes=["rbcT"])
            kb.plan("v", lambda v: v.tensor_tensor(
                out=mrbcT[:, 0:w], in0=meanT[:, 0:w], in1=rbcT[:, 0:w], op=ALU.mult),
                reads=["meanT", "rbcT"], writes=["mrbcT"])

        NPx = int(upto[4:]) if upto.startswith("pair") else (NP if upto in ("full", "attn") else 1)

        def qkv_thunks(p):
            par = p % 2
            qP, kP, vP = qPs[par], kPs[par], vPs[par]
            qn, kn, vn = f"qP{par}", f"kP{par}", f"vP{par}"
            nm0 = par * 2
            wq_b, wk_b, wv_b = wosl[par], w1sl[par], w2sl[par]
            wq_n, wk_n, wv_n = f"wosl{par}", f"w1sl{par}", f"w2sl{par}"
            for (buf, nm, srcw) in [(wq_b, wq_n, wqT), (wk_b, wk_n, wkT), (wv_b, wv_n, wvT)]:
                kb.dma("g", f"Dg_{nm}", lambda g, b=buf, sw=srcw, p=p: g.dma_start(
                    out=b.rearrange("p (c o) -> p c o", o=128),
                    in_=sw.rearrange("(c p) o -> p c o", p=128)[:, :, p * 128:(p + 1) * 128]),
                    writes=[nm])
            th = []

            for (wbuf, wname, dstbuf, dstname, scale) in [
                    (wq_b, wq_n, qP, "qP", 0.125), (wk_b, wk_n, kP, "kP", 1.0)]:
                for sub in range(S // 512):

                    def t_proj(st, wbuf=wbuf, wname=wname, sub=sub,
                               dstbuf=dstbuf, dstname=dstname, scale=scale, par=par):
                        bk = 2 * st
                        def mk(t):
                            m = None
                            for i in range(DC):
                                m = t.matmul(PB(bk), wbuf[:, i * 128:(i + 1) * 128],
                                             xbf[i][:, sub * 512:(sub + 1) * 512],
                                             start=(i == 0), stop=(i == DC - 1),
                                             skip_group_check=True)
                            return m
                        kb.plan("t", mk, reads=[wname] + [f"xbf{i}" for i in range(DC)],
                                writes=[f"pb{bk}"])
                        if scale == 1.0:
                            kb.plan("v", lambda v: v.tensor_copy(
                                out=dstbuf[:, sub * 512:(sub + 1) * 512], in_=PB(bk)),
                                reads=[f"pb{bk}"], writes=[dstname + str(par)])
                        else:
                            kb.plan("v", lambda v: v.tensor_scalar(
                                out=dstbuf[:, sub * 512:(sub + 1) * 512], in0=PB(bk),
                                scalar1=scale, scalar2=None, op0=ALU.mult),
                                reads=[f"pb{bk}"], writes=[dstname + str(par)])
                    th.append(t_proj)
            for vq in range(CT // 4):

                def t_v(st, vq=vq, wv_b=wv_b, wv_n=wv_n, vP=vP, vn=vn):
                    bk = 2 * st
                    def mkv(t):
                        m = None
                        for cl in range(4):
                            c = vq * 4 + cl
                            for i in range(DC):
                                m = t.matmul(PB(bk)[:, cl * 128:(cl + 1) * 128],
                                             xbf[i][:, c * 128:(c + 1) * 128],
                                             wv_b[:, i * 128:(i + 1) * 128],
                                             start=(i == 0), stop=(i == DC - 1),
                                             skip_group_check=True)
                        return m
                    kb.plan("t", mkv, reads=[wv_n] + [f"xbf{i}" for i in range(DC)],
                            writes=[f"pb{bk}"])
                    vv = vP.rearrange("p (c k) -> p c k", k=130)[:, vq * 4:(vq + 1) * 4, :]
                    pp = PB(bk).rearrange("p (c k) -> p c k", k=128)
                    kb.plan("v", lambda v: v.tensor_copy(
                        out=vv[:, :, 0:64], in_=pp[:, :, 0:64]),
                        reads=[f"pb{bk}"], writes=[vn])
                    kb.plan("v", lambda v: v.tensor_copy(
                        out=vv[:, :, 65:129], in_=pp[:, :, 64:128]),
                        reads=[f"pb{bk}"], writes=[vn])
                th.append(t_v)

            nsub = max(1, S // 512)
            sw = S // nsub
            for (srcbuf, srcname, col) in [(qP, qn, 0), (kP, kn, 1)]:
                sqs, nms = [], []
                for subn in range(nsub):
                    off = subn * sw
                    sq = subn % 2

                    def t_sq(st, srcbuf=srcbuf, srcname=srcname, off=off, sw=sw,
                             sq=sq):
                        kb.plan("v", lambda v: v.tensor_tensor(
                            out=sqT[:, sq * 512: sq * 512 + sw],
                            in0=srcbuf[:, off:off + sw],
                            in1=srcbuf[:, off:off + sw], op=ALU.mult),
                            reads=[srcname], writes=[f"sqT{sq}"])

                    def t_norm(st, sw=sw, col=col, first=(subn == 0), sq=sq):
                        bk = 2 * st
                        kb.plan("t", lambda t, bk=bk: t.matmul(
                            psum[0:1, bk * 512: bk * 512 + sw],
                            ones_b[:, 0:1], sqT[:, sq * 512: sq * 512 + sw],
                            start=True, stop=True, skip_group_check=True),
                            reads=[f"sqT{sq}", "ones_b"], writes=[f"pb{bk}"])
                        dst = col if first else 20
                        kb.plan("v", lambda v, bk=bk, dst=dst: v.tensor_reduce(
                            out=nmT[0:1, dst:dst + 1],
                            in_=psum[0:1, bk * 512: bk * 512 + sw],
                            axis=AX, op=ALU.max),
                            reads=[f"pb{bk}"], writes=["nmT"])
                        if not first:
                            kb.plan("v", lambda v, col=col: v.tensor_tensor(
                                out=nmT[0:1, col:col + 1], in0=nmT[0:1, col:col + 1],
                                in1=nmT[0:1, 20:21], op=ALU.max),
                                reads=["nmT"], writes=["nmT"])
                    sqs.append(t_sq)
                    nms.append(t_norm)
                if nsub >= 2:
                    for i in range(0, nsub, 2):
                        th.append(sqs[i])
                        th.append(sqs[i + 1])
                        if i >= 2:
                            th.append(nms[i - 2])
                            th.append(nms[i - 1])
                    th.append(nms[nsub - 2])
                    th.append(nms[nsub - 1])
                else:
                    th.extend(sqs)
                    th.extend(nms)

            def t_chain0(st, nm0=nm0, par=par):
                kb.plan("v", lambda v: v.tensor_tensor(
                    out=nmT[0:1, 16:17], in0=nmT[0:1, 0:1], in1=nmT[0:1, 1:2],
                    op=ALU.mult),
                    reads=["nmT"], writes=["nmT"])
                kb.plan("s", lambda s: s.activation(
                    out=nmT[0:1, 17:18], in_=nmT[0:1, 16:17], func=AF.Sqrt),
                    reads=["nmT"], writes=["nmT"])
                kb.plan("v", lambda v: v.tensor_scalar(
                    out=nmT[0:1, 16:17], in0=nmT[0:1, 17:18], scalar1=-1.0,
                    scalar2=None, op0=ALU.mult),
                    reads=["nmT"], writes=["nmT"])

            def t_chain(st, nm0=nm0, par=par):
                chain_bk = 2 * st
                kb.plan("t", lambda t: t.matmul(
                    psum[:, chain_bk * 512: chain_bk * 512 + 1],
                    ones_f[0:1, 0:128], nmT[0:1, 16:17],
                    start=True, stop=True, skip_group_check=True),
                    reads=["nmT", "ones_f"], writes=[f"pb{chain_bk}"])
                kb.plan("v", lambda v: v.tensor_copy(
                    out=negM[:, nm0:nm0 + 1],
                    in_=psum[:, chain_bk * 512: chain_bk * 512 + 1]),
                    reads=[f"pb{chain_bk}"], writes=[f"negM{par}"])
            th.append(t_chain0)
            th.append(t_chain)
            return th

        th0 = qkv_thunks(0)
        _nps = S // 512

        def ln1_stats(s):
            c0 = s * 512
            bs, bq = (s % 2) * 2, (s % 2) * 2 + 1
            for i in range(DC):
                sl = i % 2
                kb.plan("s", lambda s, i=i, c0=c0, sl=sl: s.activation(
                    out=sqT[:, sl * 512:(sl + 1) * 512], in_=xbf[i][:, c0:c0 + 512],
                    func=AF.Square),
                    reads=[f"xbf{i}"], writes=[f"sqT{sl}"])
                kb.plan("t", lambda t, i=i, c0=c0, bs=bs: t.matmul(
                    PB(bs), ones_b[:, :], xbf[i][:, c0:c0 + 512],
                    start=(i == 0), stop=(i == DC - 1)),
                    reads=[f"xbf{i}", "ones_b"], writes=[f"pb{bs}"])
                kb.plan("t", lambda t, i=i, sl=sl, bq=bq: t.matmul(
                    PB(bq), ones_b[:, :], sqT[:, sl * 512:(sl + 1) * 512],
                    start=(i == 0), stop=(i == DC - 1)),
                    reads=[f"sqT{sl}", "ones_b"], writes=[f"pb{bq}"])

        def ln1_tail(s):
            c0 = s * 512
            bs, bq = (s % 2) * 2, (s % 2) * 2 + 1
            ln_finalize(bs, bq, 512, 1.0 / D)
            for i in range(DC):
                eng = "v" if i % 2 == 0 else "g"
                lt = lt1 if i % 2 == 0 else lt1g
                ltn = "lt1" if i % 2 == 0 else "meanT"
                kb.plan(eng, lambda e, i=i, c0=c0, lt=lt: e.tensor_tensor(
                    out=lt[:], in0=xbf[i][:, c0:c0 + 512], in1=rbcT[:], op=ALU.mult),
                    reads=[f"xbf{i}", "rbcT"], writes=[ltn])
                kb.plan(eng, lambda e, i=i, c0=c0, lt=lt: e.tensor_tensor(
                    out=xbf[i][:, c0:c0 + 512], in0=lt[:], in1=mrbcT[:], op=ALU.subtract),
                    reads=[ltn, "mrbcT"], writes=[f"xbf{i}"])
            _s = s - 2
            if _s >= 0:
                for f in (th0[_s], th0[_nps + _s], th0[2 * _nps + _s]):
                    f(2)

        def warm(n):
            for _ in range(n):
                kb.plan("t", lambda t: t.matmul(
                    psum[:, 3072:3200], ones_f[:, :], ones_f[:, 0:128],
                    start=True, stop=True, skip_group_check=True),
                    reads=["ones_f"], writes=[])

        warm(24)
        ln1_stats(0)
        for s in range(_nps):
            if s + 1 < _nps:
                ln1_stats(s + 1)
            ln1_tail(s)
            warm(8)

        for _s in range(max(0, _nps - 2), _nps):
            for f in (th0[_s], th0[_nps + _s], th0[2 * _nps + _s]):
                f(2)
        for i, f in enumerate(th0[3 * _nps:]):
            f(i % 3)

        def dump_and_stop(bufs_list):
            for i, bb in enumerate(bufs_list):
                kb.dma("g", f"Ddmp{i % 2}", lambda g, bb=bb, i=i: g.dma_start(
                    out=outT[i * 128:(i + 1) * 128, :], in_=bb[:]),
                    reads=[])
            kb.barrier("g")

        if upto == "ln1":
            dump_and_stop(xbf)
            actx.close()
            blk = ctx.enter_context(nc.Block())
            blk.tensor(lambda e: [f(e) for f in kb.prog["t"]])
            blk.scalar(lambda e: [f(e) for f in kb.prog["s"]])
            blk.vector(lambda e: [f(e) for f in kb.prog["v"]])
            blk.gpsimd(lambda e: [f(e) for f in kb.prog["g"]])
            blk.sync(lambda e: [f(e) for f in kb.prog["y"]])
            return nc

        for p in range(NPx):
            par = p % 2
            qP, kP, vP = qPs[par], kPs[par], vPs[par]
            qn, kn, vn = f"qP{par}", f"kP{par}", f"vP{par}"
            nm0 = par * 2
            nxt = qkv_thunks(p + 1) if (p + 1 < NPx) else []
            nxt_i = [0]
            _nqg = NQG
            if upto.startswith("att") or upto.startswith("fin") or upto in ("finA", "finB"):
                _nqg = 2 if upto == "att2" else 1
            NB3 = CT
            pending = []

            def plan_scores(qg, c, ping, st):
                q0 = qg * QG
                banks = [f"pb{2 * st}", f"pb{2 * st + 1}"]

                def mksc(t, c=c, q0=q0, st=st, kP=kP, qP=qP):
                    m = None
                    for (hrow, boff, tpos) in [(0, 0, (0, 0)), (64, 1, (64, 0))]:
                        m = t.matmul(
                            psum[:, (2 * st + boff) * 512: (2 * st + boff + 1) * 512],
                            kP[hrow:hrow + 64, c * 128:(c + 1) * 128],
                            qP[hrow:hrow + 64, q0:q0 + QG],
                            start=True, stop=True, skip_group_check=True,
                            tile_position=tpos)
                    return m
                kb.plan("t", mksc, reads=[kn, qn], writes=banks)
                kb.plan("s", lambda s, ping=ping, st=st, nm0=nm0: s.activation(
                    out=pT[:, ping * 1024: ping * 1024 + 1024],
                    in_=psum[:, 2 * st * 512: (2 * st + 2) * 512],
                    func=AF.Exp, bias=negM[:, nm0:nm0 + 1]),
                    reads=banks + [f"negM{par}"],
                    writes=[f"pT{ping}"])

            def plan_pv(qg, c, ping):
                def mkpv(t, c=c, ping=ping, vP=vP):
                    pA = pT[:, ping * 1024: ping * 1024 + 512]
                    pB = pT[:, ping * 1024 + 512: ping * 1024 + 1024]
                    m = t.matmul(psum[0:65, 3072:3584],
                                 vP[:, c * 130: c * 130 + 65], pA,
                                 start=(c == 0), stop=(c == CT - 1),
                                 skip_group_check=True)
                    m = t.matmul(psum[0:65, 3584:4096],
                                 vP[:, c * 130 + 65: c * 130 + 130], pB,
                                 start=(c == 0), stop=(c == CT - 1),
                                 skip_group_check=True)
                    return m
                kb.plan("t", mkpv, reads=[f"pT{ping}", vn],
                        writes=["pb6", "pb7"])

            def plan_fin_stage(qg):
                kb.plan("s", lambda s: s.activation(
                    out=rbc2[64:65, 0:1024], in_=psum[64:65, 3072:4096],
                    func=AF.Ln),
                    reads=["pb6", "pb7"], writes=["rbc2"])
                kb.plan("s", lambda s: s.activation(
                    out=recT[64:65, 0:1024], in_=rbc2[64:65, 0:1024],
                    func=AF.Exp, scale=-1.0),
                    reads=["rbc2"], writes=["recT"])

            def plan_fin_rest(qg, p=None, bk=0):
                q0 = qg * QG
                pbk = psum[:, bk * 512: (bk + 1) * 512]
                pbk2 = psum[:, (bk + 1) * 512: (bk + 2) * 512]
                kb.plan("t", lambda t: t.matmul(
                    pbk, ones_b[64:65, 0:128], recT[64:65, 0:512],
                    start=True, stop=True, skip_group_check=True),
                    reads=["recT", "ones_b"], writes=[f"pb{bk}"])
                kb.plan("t", lambda t: t.matmul(
                    pbk2, ones_b[64:65, 0:128], recT[64:65, 512:1024],
                    start=True, stop=True, skip_group_check=True),
                    reads=["recT", "ones_b"], writes=[f"pb{bk + 1}"])
                kb.plan("v", lambda v: v.tensor_copy(
                    out=rbc2[:, 0:512], in_=pbk),
                    reads=[f"pb{bk}"], writes=["rbc2"])
                kb.plan("v", lambda v: v.tensor_copy(
                    out=rbc2[:, 512:1024], in_=pbk2),
                    reads=[f"pb{bk + 1}"], writes=["rbc2"])
            def plan_fin_tt(qg, p=None):
                q0 = qg * QG
                stp = qg % 2
                kb.plan("v", lambda v, stp=stp: v.tensor_tensor(
                    out=att_st[stp][0:64, 0:QG],
                    in0=psum[0:64, 3072:3584], in1=rbc2[0:64, 0:512], op=ALU.mult),
                    reads=["pb6", "rbc2"], writes=[f"ast{stp}"])
                kb.plan("v", lambda v, stp=stp: v.tensor_tensor(
                    out=att_st[stp][0:64, QG:2 * QG],
                    in0=psum[0:64, 3584:4096], in1=rbc2[0:64, 512:1024],
                    op=ALU.mult),
                    reads=["pb7", "rbc2"], writes=[f"ast{stp}"])
                kb.dma("g", f"Dg_ast{stp}", lambda g, p=p, q0=q0, stp=stp: g.dma_start(
                    out=attnS[p * 128:p * 128 + 64, q0:q0 + QG],
                    in_=att_st[stp][0:64, 0:QG]),
                    reads=[f"ast{stp}"])
                kb.dma("g", f"Dg_ast{stp}", lambda g, p=p, q0=q0, stp=stp: g.dma_start(
                    out=attnS[p * 128 + 64:(p + 1) * 128, q0:q0 + QG],
                    in_=att_st[stp][0:64, QG:2 * QG]),
                    reads=[f"ast{stp}"])

            queue = []
            qseq = [0]

            def enq(due, f):
                queue.append((due, qseq[0], f))
                qseq[0] += 1

            ib = 0
            for qg in range(_nqg):
                for b in range(NB3):
                    plan_scores(qg, b, ib % 6, ib % 3)
                    for it in sorted([x for x in queue if x[0] <= ib]):
                        it[2]()
                    queue = [x for x in queue if x[0] > ib]
                    if ib >= 6 and b < NB3 - 2 and nxt_i[0] < len(nxt):
                        nxt[nxt_i[0]]((ib + 2) % 3)
                        nxt_i[0] += 1
                    if ib % 8 == 4:
                        p0_drain(1)
                    dpv = {0: 5, 1: 4, NB3 - 1: 2}.get(b, 3)
                    enq(ib + dpv, lambda qg=qg, b=b, ping=ib % 6: plan_pv(qg, b, ping))
                    if b == NB3 - 1 and not upto.startswith("att"):
                        enq(ib + 2, lambda qg=qg: plan_fin_stage(qg))
                        enq(ib + 3, lambda qg=qg, p=p, bk=2 * ((ib + 5) % 3):
                            plan_fin_rest(qg, p, bk))
                        enq(ib + 4, lambda qg=qg, p=p: plan_fin_tt(qg, p))
                    ib += 1
            for it in sorted(queue):
                it[2]()
            queue = []
            si = 0
            while nxt_i[0] < len(nxt):
                nxt[nxt_i[0]](si % 3)
                si += 1
                nxt_i[0] += 1

        if upto != "full":
            dump_and_stop(xbf)
            actx.close()
            blk = ctx.enter_context(nc.Block())
            blk.tensor(lambda e: [f(e) for f in kb.prog["t"]])
            blk.scalar(lambda e: [f(e) for f in kb.prog["s"]])
            blk.vector(lambda e: [f(e) for f in kb.prog["v"]])
            blk.gpsimd(lambda e: [f(e) for f in kb.prog["g"]])
            blk.sync(lambda e: [f(e) for f in kb.prog["y"]])
            return nc
        p0_drain(10 ** 9)
        actx.close()

        bctx = contextlib.ExitStack()
        w3sl = [bctx.enter_context(nc.sbuf_tensor(f"w3sl{i}", [128, 2048], BF16))
                for i in range(4)]
        hB = [bctx.enter_context(nc.sbuf_tensor(f"h{i}", [128, GB], BF16)) for i in range(HC)]
        attg = [bctx.enter_context(nc.sbuf_tensor(f"attg{dc}", [128, GB], BF16))
                for dc in range(DC)]
        g1B = [bctx.enter_context(nc.sbuf_tensor(f"g1{i}", [128, GB], BF16)) for i in range(HC)]
        projb = [bctx.enter_context(nc.sbuf_tensor(f"projb{j}", [128, GB], BF16))
                 for j in range(DC)]
        ln2b = [bctx.enter_context(nc.sbuf_tensor(f"ln2b{j}", [128, GB], BF16))
                for j in range(DC)]
        xld = [bctx.enter_context(nc.sbuf_tensor(f"xld{i}", [128, GB], F32)) for i in range(2)]
        x2R = [[bctx.enter_context(nc.sbuf_tensor(f"x2R{pp}_{j}", [128, GB], F32))
                for j in range(DC)] for pp in range(2)]
        x2b = [bctx.enter_context(nc.sbuf_tensor(f"x2b{i}", [128, GB], BF16))
               for i in range(2)]
        sab = [bctx.enter_context(nc.sbuf_tensor(f"sab{i}", [128, GB], BF16)) for i in range(2)]
        outf = xld

        for stm in "tsvgy":
            kb.barrier(stm)

        def head_thunks(g_):
            gs = g_ * GB
            par = g_ % 2
            for dc in range(DC):
                kb.dma("y", f"Dy_ag{dc}", lambda y, dc=dc, gs=gs:
                       y.dma_start(out=attg[dc][:],
                                   in_=attnS[dc * 128:(dc + 1) * 128, gs:gs + GB]),
                       writes=[f"ag_{dc}"])
            th = []
            for j in range(DC):
                jp = j % 2

                def t_wo(j=j, jp=jp, par=par):
                    kb.dma("y", f"Dy_wosl{jp}", lambda y, j=j, jp=jp: y.dma_start(
                        out=wosl[jp][:], in_=woB[j]),
                        writes=[f"wosl{jp}"])

                    def mkwo(t, jp=jp):
                        m = None
                        for dc in range(DC):
                            m = t.matmul(PB(2 + jp), wosl[jp][:, dc * 128:(dc + 1) * 128],
                                         attg[dc][:],
                                         start=(dc == 0), stop=(dc == DC - 1))
                        return m
                    kb.plan("t", mkwo,
                            reads=[f"wosl{jp}"] + [f"ag_{dc}" for dc in range(DC)],
                            writes=[f"pb{2 + jp}"])
                    kb.plan("v", lambda v, j=j, jp=jp: v.tensor_scalar(
                        out=projb[j][:], in0=PB(2 + jp), scalar1=bo_sb[:, j:j + 1],
                        scalar2=None, op0=ALU.add),
                        reads=[f"pb{2 + jp}", "bo_sb"], writes=[f"projb{j}"])
                th.append(t_wo)
            for j in range(DC):
                xp = j % 2

                def t_stsq(j=j, xp=xp):
                    kb.plan("s", lambda s, j=j, xp=xp: s.activation(
                        out=sqT[:, xp * 512:(xp + 1) * 512], in_=projb[j][:],
                        func=AF.Square),
                        reads=[f"projb{j}"], writes=[f"sqT{xp}"])

                def t_st(j=j, xp=xp):
                    kb.plan("t", lambda t, j=j: t.matmul(
                        PB(4), ones_b[:, :], projb[j][:],
                        start=(j == 0), stop=(j == DC - 1), skip_group_check=True),
                        reads=[f"projb{j}", "ones_b"], writes=["pb4"])
                    kb.plan("t", lambda t, j=j, xp=xp: t.matmul(
                        PB(5), ones_b[:, :], sqT[:, xp * 512:(xp + 1) * 512],
                        start=(j == 0), stop=(j == DC - 1), skip_group_check=True),
                        reads=[f"sqT{xp}", "ones_b"], writes=["pb5"])
                th.append(t_stsq)
                th.append(t_st)
            th.append(lambda: ln_finalize(4, 5, GB, 1.0 / D))
            for j in range(DC):

                def t_x2a(j=j, par=par, gs=gs):
                    jp = j % 2
                    kb.dma("y", f"Dy_xld{jp}", lambda y, j=j, jp=jp, gs=gs: y.dma_start(
                        out=xld[jp][:], in_=xT[j * 128:(j + 1) * 128, gs:gs + GB]),
                        writes=[f"xld{jp}"])
                    eng = "v" if j % 2 == 0 else "g"
                    lt = lt1 if j % 2 == 0 else lt1g
                    ltn = "lt1" if j % 2 == 0 else "meanT"
                    kb.plan(eng, lambda e, j=j, lt=lt: e.tensor_tensor(
                        out=lt[:], in0=projb[j][:], in1=rbcT[:], op=ALU.mult),
                        reads=[f"projb{j}", "rbcT"], writes=[ltn])
                    kb.plan(eng, lambda e, lt=lt: e.tensor_tensor(
                        out=lt[:], in0=lt[:], in1=mrbcT[:], op=ALU.subtract),
                        reads=[ltn, "mrbcT"], writes=[ltn])
                    kb.plan(eng, lambda e, j=j, jp=jp, par=par, lt=lt: e.tensor_tensor(
                        out=x2R[par][j][:], in0=lt[:], in1=xld[jp][:], op=ALU.add),
                        reads=[ltn, f"xld{jp}"], writes=[f"x2R{par}_{j}"])
                    xp = j % 2
                    kb.plan("v", lambda v, j=j, xp=xp, par=par: v.tensor_copy(
                        out=x2b[xp][:], in_=x2R[par][j][:]),
                        reads=[f"x2R{par}_{j}"], writes=[f"x2b{xp}"])
                    kb.plan("s", lambda s, xp=xp, j=j: s.activation(
                        out=sqT[:, xp * 512:(xp + 1) * 512], in_=x2b[xp][:],
                        func=AF.Square),
                        reads=[f"x2b{xp}"], writes=[f"sqT{xp}"])

                def t_x2b(j=j):
                    xp = j % 2
                    kb.plan("t", lambda t, j=j, xp=xp: t.matmul(
                        PB(4), ones_b[:, :], x2b[xp][:],
                        start=(j == 0), stop=(j == DC - 1), skip_group_check=True),
                        reads=[f"x2b{xp}", "ones_b"], writes=["pb4"])
                    kb.plan("t", lambda t, j=j, xp=xp: t.matmul(
                        PB(5), ones_b[:, :], sqT[:, xp * 512:(xp + 1) * 512],
                        start=(j == 0), stop=(j == DC - 1), skip_group_check=True),
                        reads=[f"sqT{xp}", "ones_b"], writes=["pb5"])
                th.append(t_x2a)
                th.append(t_x2b)
            th.append(lambda: ln_finalize(4, 5, GB, 1.0 / D))
            for j in range(DC):

                def t_ln2(j=j, par=par):
                    kb.plan("v", lambda v, j=j, par=par: v.tensor_tensor(
                        out=lt1[:], in0=x2R[par][j][:], in1=rbcT[:], op=ALU.mult),
                        reads=[f"x2R{par}_{j}", "rbcT"], writes=["lt1"])
                    kb.plan("v", lambda v, j=j: v.tensor_tensor(
                        out=ln2b[j][:], in0=lt1[:], in1=mrbcT[:], op=ALU.subtract),
                        reads=["lt1", "mrbcT"], writes=[f"ln2b{j}"])
                th.append(t_ln2)
            return th

        for f in head_thunks(0):
            f()
        prev_out = []
        for g_ in range(NG):
            gs = g_ * GB
            par = g_ % 2
            nxt_h = head_thunks(g_ + 1) if g_ + 1 < NG else []
            hi_ = [0]

            def hdrain(nxt_h=nxt_h, hi_=hi_):
                if hi_[0] < len(nxt_h):
                    nxt_h[hi_[0]]()
                    hi_[0] += 1

            def w12_load(jj):
                jp = jj % 3
                kb.dma("y", f"Dy_w1sl{jp}", lambda y, jj=jj, jp=jp: y.dma_start(
                    out=w1sl[jp][:], in_=w1B[jj]),
                    writes=[f"w1sl{jp}"])
                kb.dma("y", f"Dy_w2sl{jp}", lambda y, jj=jj, jp=jp: y.dma_start(
                    out=w2sl[jp][:], in_=w2B[jj]),
                    writes=[f"w2sl{jp}"])
            w12_load(0)
            w12_load(1)
            po_i = [0]
            for jj in range(HC):
                jp = jj % 3
                bka, bkb = (jj % 2) * 2, (jj % 2) * 2 + 1
                if jj + 2 < HC:
                    w12_load(jj + 2)
                if jj % 4 == 1 and po_i[0] < len(prev_out):
                    prev_out[po_i[0]]()
                    po_i[0] += 1

                def mkw1(t, jp=jp, bka=bka):
                    m = None
                    for i in range(DC):
                        m = t.matmul(PB(bka), w1sl[jp][:, i * 128:(i + 1) * 128], ln2b[i][:],
                                     start=(i == 0), stop=(i == DC - 1))
                    return m
                kb.plan("t", mkw1, reads=[f"w1sl{jp}"] + [f"ln2b{i}" for i in range(DC)],
                        writes=[f"pb{bka}"])

                def mkw2(t, jp=jp, bkb=bkb):
                    m = None
                    for i in range(DC):
                        m = t.matmul(PB(bkb), w2sl[jp][:, i * 128:(i + 1) * 128], ln2b[i][:],
                                     start=(i == 0), stop=(i == DC - 1))
                    return m
                kb.plan("t", mkw2, reads=[f"w2sl{jp}"] + [f"ln2b{i}" for i in range(DC)],
                        writes=[f"pb{bkb}"])
                sp = jj % 2
                kb.plan("s", lambda s, jj=jj, sp=sp, bka=bka: s.activation(
                    out=sab[sp][:], in_=PB(bka), func=AF.Silu, scale=w1s_sb[:, jj:jj + 1]),
                    reads=[f"pb{bka}", "w1s_sb"], writes=[f"sab{sp}"])
                kb.plan("v", lambda v, jj=jj, sp=sp, bkb=bkb: v.scalar_tensor_tensor(
                    out=hB[jj][:], in0=PB(bkb), scalar=w2s_sb[:, jj:jj + 1],
                    in1=sab[sp][:], op0=ALU.mult, op1=ALU.mult),
                    reads=[f"pb{bkb}", "w2s_sb", f"sab{sp}"], writes=[f"h{jj}"])

            def h1_load(jj):
                jp = jj % 2
                for hh in range(2):
                    kb.dma("y", f"Dy_h1sl{jp * 2 + hh}", lambda y, jj=jj, jp=jp, hh=hh:
                           y.dma_start(out=h1sl[jp * 2 + hh][:],
                                       in_=h1B[jj, :, hh * 2048:(hh + 1) * 2048]),
                           writes=[f"h1sl{jp * 2 + hh}"])
            h1_load(0)
            for jj in range(HC):
                jp = jj % 2
                bk = 4 + jp
                if jj + 1 < HC:
                    h1_load(jj + 1)

                def mkh1(t, jp=jp, bk=bk):
                    m = None
                    for i in range(HC):
                        sl = h1sl[jp * 2 + (i // 16)]
                        m = t.matmul(PB(bk), sl[:, (i % 16) * 128:(i % 16 + 1) * 128], hB[i][:],
                                     start=(i == 0), stop=(i == HC - 1))
                    return m
                kb.plan("t", mkh1, reads=[f"h1sl{jp * 2}", f"h1sl{jp * 2 + 1}"]
                        + [f"h{i}" for i in range(HC)], writes=[f"pb{bk}"])
                kb.plan("s", lambda s, jj=jj, bk=bk: s.activation(
                    out=g1B[jj][:], in_=PB(bk), func=AF.Gelu, scale=h1s_sb[:, jj:jj + 1]),
                    reads=[f"pb{bk}", "h1s_sb"], writes=[f"g1{jj}"])

            def h2_load(jj):
                jp = jj % 2
                for hh in range(2):
                    kb.dma("g", f"Dg_h2sl{jp * 2 + hh}", lambda g, jj=jj, jp=jp, hh=hh:
                           g.dma_start(out=h2sl[jp * 2 + hh][:],
                                       in_=h2B[jj, :, hh * 2048:(hh + 1) * 2048]),
                           writes=[f"h2sl{jp * 2 + hh}"])
            h2_load(0)
            for jj in range(HC):
                jp = jj % 2
                bk = 6 + jp
                if jj + 1 < HC:
                    h2_load(jj + 1)

                def mkh2(t, jp=jp, bk=bk):
                    m = None
                    for i in range(HC):
                        sl = h2sl[jp * 2 + (i // 16)]
                        m = t.matmul(PB(bk), sl[:, (i % 16) * 128:(i % 16 + 1) * 128], g1B[i][:],
                                     start=(i == 0), stop=(i == HC - 1))
                    return m
                kb.plan("t", mkh2, reads=[f"h2sl{jp * 2}", f"h2sl{jp * 2 + 1}"]
                        + [f"g1{i}" for i in range(HC)], writes=[f"pb{bk}"])
                kb.plan("s", lambda s, jj=jj, bk=bk: s.activation(
                    out=hB[jj][:], in_=PB(bk), func=AF.Gelu, scale=h2s_sb[:, jj:jj + 1]),
                    reads=[f"pb{bk}", "h2s_sb"], writes=[f"h{jj}"])
                hdrain()
                if jj >= 16:
                    hdrain()

            def w3_load(j):
                jp = j % 2
                for hh in range(2):
                    kb.dma("g", f"Dg_w3sl{jp * 2 + hh}", lambda g, j=j, jp=jp, hh=hh:
                           g.dma_start(out=w3sl[jp * 2 + hh][:],
                                       in_=w3B[j, :, hh * 2048:(hh + 1) * 2048]),
                           writes=[f"w3sl{jp * 2 + hh}"])
            w3_load(0)
            for j in range(DC):
                hdrain()
                jp = j % 2
                bk = 2 + jp
                if j + 1 < DC:
                    w3_load(j + 1)

                def mkw3(t, jp=jp, bk=bk):
                    m = None
                    for i in range(HC):
                        sl = w3sl[jp * 2 + (i // 16)]
                        m = t.matmul(PB(bk), sl[:, (i % 16) * 128:(i % 16 + 1) * 128], hB[i][:],
                                     start=(i == 0), stop=(i == HC - 1))
                    return m
                kb.plan("t", mkw3, reads=[f"w3sl{jp * 2}", f"w3sl{jp * 2 + 1}"]
                        + [f"h{i}" for i in range(HC)], writes=[f"pb{bk}"])
                kb.plan("v", lambda v, j=j, bk=bk: v.tensor_scalar(
                    out=projb[j][:], in0=PB(bk), scalar1=w3s_sb[:, j:j + 1],
                    scalar2=None, op0=ALU.mult),
                    reads=[f"pb{bk}", "w3s_sb"], writes=[f"projb{j}"])
            while hi_[0] < len(nxt_h):
                nxt_h[hi_[0]]()
                hi_[0] += 1
            for j in range(DC):
                kb.plan("s", lambda s, j=j: s.activation(
                    out=hB[j][:], in_=projb[j][:], func=AF.Square),
                    reads=[f"projb{j}"], writes=[f"h{j}"])
            for j in range(DC):
                kb.plan("t", lambda t, j=j: t.matmul(
                    PB(2), ones_b[:, :], projb[j][:], start=(j == 0), stop=(j == DC - 1)),
                    reads=[f"projb{j}", "ones_b"], writes=["pb2"])
                kb.plan("t", lambda t, j=j: t.matmul(
                    PB(3), ones_b[:, :], hB[j][:], start=(j == 0), stop=(j == DC - 1)),
                    reads=[f"h{j}", "ones_b"], writes=["pb3"])
            ln_finalize(2, 3, GB, 1.0 / D)

            def t_out(j, par=par, gs=gs):
                jp = j % 2
                kb.plan("v", lambda v, j=j: v.tensor_tensor(
                    out=lt1[:], in0=projb[j][:], in1=rbcT[:], op=ALU.mult),
                    reads=[f"projb{j}", "rbcT"], writes=["lt1"])
                kb.plan("v", lambda v: v.tensor_tensor(
                    out=lt1[:], in0=lt1[:], in1=mrbcT[:], op=ALU.subtract),
                    reads=["lt1", "mrbcT"], writes=["lt1"])
                kb.plan("v", lambda v, j=j, jp=jp, par=par: v.tensor_tensor(
                    out=outf[jp][:], in0=lt1[:], in1=x2R[par][j][:], op=ALU.add),
                    reads=["lt1", f"x2R{par}_{j}"], writes=[f"xld{jp}"])
                kb.dma("g", f"Dg_out{jp}", lambda g, j=j, jp=jp, gs=gs: g.dma_start(
                    out=outT[j * 128:(j + 1) * 128, gs:gs + GB], in_=outf[jp][:]),
                    reads=[f"xld{jp}"])
            if g_ + 1 < NG:
                prev_out = [lambda j=j, t_out=t_out: t_out(j) for j in range(DC)]
            else:
                for j in range(DC):
                    t_out(j)

        bctx.close()
        kb.barrier("g", ["Dg_out0", "Dg_out1"])

        block = ctx.enter_context(nc.Block())
        block.tensor(lambda e: [f(e) for f in kb.prog["t"]])
        block.scalar(lambda e: [f(e) for f in kb.prog["s"]])
        block.vector(lambda e: [f(e) for f in kb.prog["v"]])
        block.gpsimd(lambda e: [f(e) for f in kb.prog["g"]])
        block.sync(lambda e: [f(e) for f in kb.prog["y"]])

    return nc


def kernel(L0_attn_x, L0_attn_Wq, L0_attn_Wk, L0_attn_Wv, L0_attn_Wo, L0_attn_bo,
           L0_ffn_w1_w, L0_ffn_w1_s, L0_ffn_w2_w, L0_ffn_w2_s,
           L0_ffn_h1_w, L0_ffn_h1_s, L0_ffn_h2_w, L0_ffn_h2_s,
           L0_ffn_w3_w, L0_ffn_w3_s):
    x = np.asarray(L0_attn_x)
    B, S, _ = x.shape
    assert B == NCORES and x.shape[2] == D

    cT = lambda a: np.ascontiguousarray(np.asarray(a, dtype=np.float32).T)
    shared = {
        "wqT": cT(L0_attn_Wq), "wkT": cT(L0_attn_Wk),
        "wvT": cT(L0_attn_Wv), "woT": cT(L0_attn_Wo),
        "bo_r": np.ascontiguousarray(np.asarray(L0_attn_bo, np.float32).reshape(DC, 128).T),
        "w1T": cT(L0_ffn_w1_w), "w2T": cT(L0_ffn_w2_w), "w3T": cT(L0_ffn_w3_w),
        "h1T": cT(L0_ffn_h1_w), "h2T": cT(L0_ffn_h2_w),
        "w1s": np.ascontiguousarray(np.asarray(L0_ffn_w1_s, np.float32).reshape(HC, 128).T),
        "w2s": np.ascontiguousarray(np.asarray(L0_ffn_w2_s, np.float32).reshape(HC, 128).T),
        "h1s": np.ascontiguousarray(np.asarray(L0_ffn_h1_s, np.float32).reshape(HC, 128).T),
        "h2s": np.ascontiguousarray(np.asarray(L0_ffn_h2_s, np.float32).reshape(HC, 128).T),
        "w3s": np.ascontiguousarray(np.asarray(L0_ffn_w3_s, np.float32).reshape(DC, 128).T),
    }
    in_maps = []
    for c in range(NCORES):
        m = dict(shared)
        m["xT"] = np.ascontiguousarray(x[c].T.astype(np.float32))
        in_maps.append(m)

    nc = build(S)
    res = run_bass_kernel_spmd(nc, in_maps, core_ids=list(range(NCORES)))
    out = np.stack([np.asarray(r["outT"], dtype=np.float32).T for r in res.results], axis=0)
    return out


# version 51
# speedup vs baseline: 1.1750x; 1.1750x over previous
import numpy as np
import contextlib
import concourse.bass as bass
import concourse.mybir as mybir
from concourse.bass_utils import run_bass_kernel_spmd

F32 = mybir.dt.float32
BF16 = mybir.dt.bfloat16
AF = mybir.ActivationFunctionType
ALU = mybir.AluOpType
AX = mybir.AxisListType.X

D = 1024
DC = 8
HID = 4096
HC = 32
NP = 8
QG = 512
GB = 512
EPS = 1e-5
NCORES = 8

STREAM_OF = {"t": "t", "s": "s", "v": "v", "g": "g", "y": "y",
             "dy": "y", "dg": "g", "ds": "s"}


class KB:

    def __init__(self, nc):
        self.nc = nc
        self.h = {}
        self.n = {}
        self.waited = {}
        self.bufs = {}
        self.prog = {k: [] for k in "tsvgy"}

    def mksem(self, ctx, *keys):
        self.ctx = ctx
        for k in keys:
            self.h[k] = ctx.enter_context(self.nc.semaphore(k))
            self.n[k] = 0

    def getsem(self, k):
        if k not in self.h:
            self.h[k] = self.ctx.enter_context(self.nc.semaphore(k))
            self.n[k] = 0
        return self.h[k]

    def settle(self, bufnames, key):
        for b in bufnames:
            self.bufs[b] = [(key, self.n[key]), []]

    def dma(self, qkey, semkey, maker, reads=(), writes=()):
        self.getsem(semkey)
        return self.plan(semkey, maker, reads, writes, inc=16, stream=qkey)

    def plan(self, key, makers, reads=(), writes=(), inc=1, stream=None):
        self.getsem(key)
        stream = stream or STREAM_OF[key]
        waits = {}
        for b in reads:
            s = self.bufs.get(b)
            if s and s[0]:
                k2, v2 = s[0]
                waits[k2] = max(waits.get(k2, 0), v2)
        for b in writes:
            s = self.bufs.get(b)
            if s:
                if s[0]:
                    k2, v2 = s[0]
                    waits[k2] = max(waits.get(k2, 0), v2)
                for (k2, v2) in s[1]:
                    waits[k2] = max(waits.get(k2, 0), v2)
        w = self.waited.setdefault(stream, {})
        fw = []
        for k2, v2 in waits.items():
            if v2 > 0 and w.get(k2, 0) < v2:
                fw.append((self.h[k2], v2))
                w[k2] = v2
        self.n[key] += inc
        cnt = self.n[key]
        for b in reads:
            self.bufs.setdefault(b, [None, []])[1].append((key, cnt))
        for b in writes:
            self.bufs[b] = [(key, cnt), []]
        sem = self.h[key]
        if not isinstance(makers, (list, tuple)):
            makers = [makers]

        def go(eng, fw=fw, makers=makers, sem=sem, inc=inc):
            for (hh, vv) in fw:
                eng.wait_ge(hh, vv)
            ins = None
            for m in makers:
                ins = m(eng)
            ins.then_inc(sem, inc)
        self.prog[stream].append(go)
        return cnt

    def barrier(self, stream, keys=None):
        keys = list(self.h.keys()) if keys is None else keys
        w = self.waited.setdefault(stream, {})
        fw = []
        for k in keys:
            if self.n[k] > 0 and w.get(k, 0) < self.n[k]:
                fw.append((self.h[k], self.n[k]))
                w[k] = self.n[k]

        def go(eng, fw=fw):
            for (hh, vv) in fw:
                eng.wait_ge(hh, vv)
        self.prog[stream].append(go)

    def final_wait(self, stream, key):
        sem, val = self.h[key], self.n[key]

        def go(eng):
            eng.wait_ge(sem, val)
        self.prog[stream].append(go)


def build(S=4096, upto="full"):
    CT = S // 128
    NQG = S // QG
    NG = S // GB
    NB = (CT + 5) // 6

    nc = bass.Bass("TRN2", target_bir_lowering=False)
    dram = lambda n, sh, dt=F32, kind="ExternalInput": nc.dram_tensor(n, sh, dt, kind=kind).ap()

    xT = dram("xT", [D, S])
    wqT = dram("wqT", [D, D]); wkT = dram("wkT", [D, D])
    wvT = dram("wvT", [D, D]); woT = dram("woT", [D, D])
    bo_r = dram("bo_r", [128, DC])
    w1T = dram("w1T", [D, HID]); w2T = dram("w2T", [D, HID])
    w3T = dram("w3T", [HID, D])
    h1T = dram("h1T", [HID, HID]); h2T = dram("h2T", [HID, HID])
    w1s = dram("w1s", [128, HC]); w2s = dram("w2s", [128, HC])
    h1s = dram("h1s", [128, HC]); h2s = dram("h2s", [128, HC])
    w3s = dram("w3s", [128, DC])
    outT = dram("outT", [D, S], kind="ExternalOutput")
    w1B = dram("w1B", [HC, 128, D], BF16, kind="Internal")
    w2B = dram("w2B", [HC, 128, D], BF16, kind="Internal")
    w3B = dram("w3B", [DC, 128, HID], BF16, kind="Internal")
    h1B = dram("h1B", [HC, 128, HID], BF16, kind="Internal")
    h2B = dram("h2B", [HC, 128, HID], BF16, kind="Internal")
    woB = dram("woB", [DC, 128, D], BF16, kind="Internal")
    attnS = dram("attnS", [D, S], BF16, kind="ExternalOutput")

    ctx = contextlib.ExitStack()
    with ctx:
        kb = KB(nc)
        kb.mksem(ctx, "t", "s", "v", "g")
        sb = lambda n, sh, dt=F32: ctx.enter_context(nc.sbuf_tensor(n, sh, dt))

        ones_b = sb("ones_b", [128, 128], BF16)
        ones_f = sb("ones_f", [128, 128], F32)
        epsT = sb("epsT", [128, 1], F32)
        selA = sb("selA", [128, 1], BF16)
        selB = sb("selB", [128, 1], BF16)
        bo_sb = sb("bo_sb", [128, DC], F32)
        w1s_sb = sb("w1s_sb", [128, HC], F32)
        w2s_sb = sb("w2s_sb", [128, HC], F32)
        h1s_sb = sb("h1s_sb", [128, HC], F32)
        h2s_sb = sb("h2s_sb", [128, HC], F32)
        w3s_sb = sb("w3s_sb", [128, DC], F32)
        wosl = [sb(f"wosl{i}", [128, D], BF16) for i in range(2)]
        w1sl = [sb(f"w1sl{i}", [128, D], BF16) for i in range(3)]
        w2sl = [sb(f"w2sl{i}", [128, D], BF16) for i in range(3)]
        h1sl = [sb(f"h1sl{i}", [128, 2048], BF16) for i in range(4)]
        h2sl = [sb(f"h2sl{i}", [128, 2048], BF16) for i in range(4)]
        meanT = sb("meanT", [128, 512], F32)
        t1T = sb("t1T", [128, 512], F32)
        t2T = sb("t2T", [128, 512], F32)
        rbcT = sb("rbcT", [128, 512], F32)
        mrbcT = sb("mrbcT", [128, 512], F32)
        lt1 = sb("lt1", [128, 512], F32)
        lt1g = meanT
        sqT = sb("sqT", [128, 1024], BF16)

        psum = ctx.enter_context(nc.psum_tensor("psum", [128, 4096], F32))
        PB = lambda k: psum[:, 512 * k:512 * (k + 1)]

        kb.plan("v", lambda v: v.memset(ones_b[:], 1.0), writes=["ones_b"])
        kb.plan("v", lambda v: v.memset(ones_f[:], 1.0), writes=["ones_f"])
        kb.plan("v", lambda v: v.memset(epsT[:], EPS), writes=["epsT"])
        kb.plan("v", lambda v: v.memset(selA[:], 0.0), writes=["selA"])
        kb.plan("v", lambda v: v.memset(selA[0:64, :], 1.0), writes=["selA"])
        kb.plan("v", lambda v: v.memset(selB[:], 0.0), writes=["selB"])
        kb.plan("v", lambda v: v.memset(selB[64:128, :], 1.0), writes=["selB"])
        _scl = [(bo_sb, bo_r, "bo_sb"), (w1s_sb, w1s, "w1s_sb"),
                (w2s_sb, w2s, "w2s_sb"), (h1s_sb, h1s, "h1s_sb"),
                (h2s_sb, h2s, "h2s_sb"), (w3s_sb, w3s, "w3s_sb")]
        for (dst, src, nm) in _scl:
            kb.dma("y", "dsc", lambda y, d=dst, s=src: y.dma_start(out=d[:], in_=s),
                   writes=[nm])
        kb.settle([nm for (_, _, nm) in _scl], "dsc")

        actx = contextlib.ExitStack()
        xbf = [actx.enter_context(nc.sbuf_tensor(f"xbf{i}", [128, S], BF16)) for i in range(DC)]
        qPs = [actx.enter_context(nc.sbuf_tensor(f"qP{i}", [128, S], BF16)) for i in range(2)]
        kPs = [actx.enter_context(nc.sbuf_tensor(f"kP{i}", [128, S], BF16)) for i in range(2)]
        vPs = [actx.enter_context(nc.sbuf_tensor(f"vP{i}", [128, (S // 128) * 130], BF16))
               for i in range(2)]
        for i in range(2):
            kb.plan("v", lambda v, i=i: v.memset(
                vPs[i].rearrange("p (c k) -> p c k", k=130)[:, :, 64:65], 1.0),
                writes=[f"vP{i}"])
            kb.plan("v", lambda v, i=i: v.memset(
                vPs[i].rearrange("p (c k) -> p c k", k=130)[:, :, 129:130], 1.0),
                writes=[f"vP{i}"])
        pT = actx.enter_context(nc.sbuf_tensor("pT", [128, 6 * 1024], BF16))
        recT = actx.enter_context(nc.sbuf_tensor("recT", [128, 1024], BF16))
        rbc2 = actx.enter_context(nc.sbuf_tensor("rbc2", [128, 1024], F32))
        nmT = actx.enter_context(nc.sbuf_tensor("nmT", [128, 24], F32))
        negM = actx.enter_context(nc.sbuf_tensor("negM", [128, 4], F32))
        att_st = [actx.enter_context(nc.sbuf_tensor(f"att_st{i}", [128, 2 * QG], BF16))
                  for i in range(2)]

        for i in range(DC):
            kb.dma("g", f"Dgx{i}", lambda g, i=i: g.dma_start(
                out=xbf[i][:], in_=xT[i * 128:(i + 1) * 128, :]),
                writes=[f"xbf{i}"])

        p0_jobs = []
        for (srcT, dstB, rows, nm) in [(woT, woB, D, "woB"), (w1T, w1B, D, "w1B"),
                                       (w2T, w2B, D, "w2B"), (w3T, w3B, HID, "w3B"),
                                       (h1T, h1B, HID, "h1B"), (h2T, h2B, HID, "h2B")]:
            cols = srcT.shape[1]
            cw = min(2048, cols)
            for r in range(rows // 128):
                for cc in range(cols // cw):
                    p0_jobs.append((srcT, dstB, nm, r, cc, cw))
        p0_state = {"i": 0, "slot": 0}

        def p0_drain(k):
            end = min(len(p0_jobs), p0_state["i"] + k)
            while p0_state["i"] < end:
                (srcT, dstB, nm, r, cc, cw) = p0_jobs[p0_state["i"]]
                sl = p0_state["slot"] % 4
                p0_state["slot"] += 1
                p0_state["i"] += 1
                buf = h1sl[sl] if sl < 2 else h2sl[sl - 2]
                bname = f"h1sl{sl}" if sl < 2 else f"h2sl{sl - 2}"
                ns = cw // 128
                kb.dma("g", f"Dg_{bname}", lambda g, b=buf, s=srcT, r=r, cc=cc, cw=cw:
                       g.dma_start(out=b[:, 0:cw],
                                   in_=s[r * 128:(r + 1) * 128, cc * cw:(cc + 1) * cw]),
                       writes=[bname])
                kb.dma("y", f"Dy0_{bname}", lambda y, b=buf, d=dstB, r=r, cc=cc, cw=cw, ns=ns:
                       y.dma_start(
                           out=d.rearrange("s p f -> p s f")
                           [:, cc * ns:(cc + 1) * ns, r * 128:(r + 1) * 128],
                           in_=b[:, 0:cw].rearrange("p (s oc) -> p s oc", oc=128)),
                       reads=[bname])

        def ln_finalize(sum_b, sq_b, w, inv):
            kb.plan("v", lambda v: v.tensor_scalar(
                out=meanT[:, 0:w], in0=PB(sum_b)[:, 0:w], scalar1=inv,
                scalar2=None, op0=ALU.mult),
                reads=[f"pb{sum_b}"], writes=["meanT"])
            kb.plan("v", lambda v: v.tensor_tensor(
                out=t1T[:, 0:w], in0=meanT[:, 0:w], in1=meanT[:, 0:w], op=ALU.mult),
                reads=["meanT"], writes=["t1T"])
            kb.plan("v", lambda v: v.scalar_tensor_tensor(
                out=t2T[:, 0:w], in0=PB(sq_b)[:, 0:w], scalar=inv, in1=t1T[:, 0:w],
                op0=ALU.mult, op1=ALU.subtract),
                reads=[f"pb{sq_b}", "t1T"], writes=["t2T"])
            kb.plan("s", lambda s: s.activation(
                out=t1T[:, 0:w], in_=t2T[:, 0:w], func=AF.Ln, bias=epsT[:, 0:1]),
                reads=["t2T", "epsT"], writes=["t1T"])
            kb.plan("s", lambda s: s.activation(
                out=rbcT[:, 0:w], in_=t1T[:, 0:w], func=AF.Exp, scale=-0.5),
                reads=["t1T"], writes=["rbcT"])
            kb.plan("v", lambda v: v.tensor_tensor(
                out=mrbcT[:, 0:w], in0=meanT[:, 0:w], in1=rbcT[:, 0:w], op=ALU.mult),
                reads=["meanT", "rbcT"], writes=["mrbcT"])

        NPx = int(upto[4:]) if upto.startswith("pair") else (NP if upto in ("full", "attn") else 1)

        def qkv_thunks(p):
            par = p % 2
            qP, kP, vP = qPs[par], kPs[par], vPs[par]
            qn, kn, vn = f"qP{par}", f"kP{par}", f"vP{par}"
            nm0 = par * 2
            wq_b, wk_b, wv_b = wosl[par], w1sl[par], w2sl[par]
            wq_n, wk_n, wv_n = f"wosl{par}", f"w1sl{par}", f"w2sl{par}"
            for (buf, nm, srcw) in [(wq_b, wq_n, wqT), (wk_b, wk_n, wkT), (wv_b, wv_n, wvT)]:
                kb.dma("g", f"Dg_{nm}", lambda g, b=buf, sw=srcw, p=p: g.dma_start(
                    out=b.rearrange("p (c o) -> p c o", o=128),
                    in_=sw.rearrange("(c p) o -> p c o", p=128)[:, :, p * 128:(p + 1) * 128]),
                    writes=[nm])
            th = []

            for (wbuf, wname, dstbuf, dstname, scale) in [
                    (wq_b, wq_n, qP, "qP", 0.125), (wk_b, wk_n, kP, "kP", 1.0)]:
                for sub in range(S // 512):

                    def t_proj(st, wbuf=wbuf, wname=wname, sub=sub,
                               dstbuf=dstbuf, dstname=dstname, scale=scale, par=par):
                        bk = 2 * st
                        def mk(t):
                            m = None
                            for i in range(DC):
                                m = t.matmul(PB(bk), wbuf[:, i * 128:(i + 1) * 128],
                                             xbf[i][:, sub * 512:(sub + 1) * 512],
                                             start=(i == 0), stop=(i == DC - 1),
                                             skip_group_check=True)
                            return m
                        kb.plan("t", mk, reads=[wname] + [f"xbf{i}" for i in range(DC)],
                                writes=[f"pb{bk}"])
                        if scale == 1.0:
                            kb.plan("v", lambda v: v.tensor_copy(
                                out=dstbuf[:, sub * 512:(sub + 1) * 512], in_=PB(bk)),
                                reads=[f"pb{bk}"], writes=[dstname + str(par)])
                        else:
                            kb.plan("v", lambda v: v.tensor_scalar(
                                out=dstbuf[:, sub * 512:(sub + 1) * 512], in0=PB(bk),
                                scalar1=scale, scalar2=None, op0=ALU.mult),
                                reads=[f"pb{bk}"], writes=[dstname + str(par)])
                    th.append(t_proj)
            for vq in range(CT // 4):

                def t_v(st, vq=vq, wv_b=wv_b, wv_n=wv_n, vP=vP, vn=vn):
                    bk = 2 * st
                    def mkv(t):
                        m = None
                        for cl in range(4):
                            c = vq * 4 + cl
                            for i in range(DC):
                                m = t.matmul(PB(bk)[:, cl * 128:(cl + 1) * 128],
                                             xbf[i][:, c * 128:(c + 1) * 128],
                                             wv_b[:, i * 128:(i + 1) * 128],
                                             start=(i == 0), stop=(i == DC - 1),
                                             skip_group_check=True)
                        return m
                    kb.plan("t", mkv, reads=[wv_n] + [f"xbf{i}" for i in range(DC)],
                            writes=[f"pb{bk}"])
                    vv = vP.rearrange("p (c k) -> p c k", k=130)[:, vq * 4:(vq + 1) * 4, :]
                    pp = PB(bk).rearrange("p (c k) -> p c k", k=128)
                    kb.plan("v", lambda v: v.tensor_copy(
                        out=vv[:, :, 0:64], in_=pp[:, :, 0:64]),
                        reads=[f"pb{bk}"], writes=[vn])
                    kb.plan("v", lambda v: v.tensor_copy(
                        out=vv[:, :, 65:129], in_=pp[:, :, 64:128]),
                        reads=[f"pb{bk}"], writes=[vn])
                th.append(t_v)

            nsub = max(1, S // 512)
            sw = S // nsub
            for (srcbuf, srcname, col) in [(qP, qn, 0), (kP, kn, 1)]:
                sqs, nms = [], []
                for subn in range(nsub):
                    off = subn * sw
                    sq = subn % 2

                    def t_sq(st, srcbuf=srcbuf, srcname=srcname, off=off, sw=sw,
                             sq=sq):
                        kb.plan("v", lambda v: v.tensor_tensor(
                            out=sqT[:, sq * 512: sq * 512 + sw],
                            in0=srcbuf[:, off:off + sw],
                            in1=srcbuf[:, off:off + sw], op=ALU.mult),
                            reads=[srcname], writes=[f"sqT{sq}"])

                    def t_norm(st, sw=sw, col=col, first=(subn == 0), sq=sq):
                        bk = 2 * st
                        kb.plan("t", lambda t, bk=bk: t.matmul(
                            psum[0:1, bk * 512: bk * 512 + sw],
                            ones_b[:, 0:1], sqT[:, sq * 512: sq * 512 + sw],
                            start=True, stop=True, skip_group_check=True),
                            reads=[f"sqT{sq}", "ones_b"], writes=[f"pb{bk}"])
                        dst = col if first else 20
                        kb.plan("v", lambda v, bk=bk, dst=dst: v.tensor_reduce(
                            out=nmT[0:1, dst:dst + 1],
                            in_=psum[0:1, bk * 512: bk * 512 + sw],
                            axis=AX, op=ALU.max),
                            reads=[f"pb{bk}"], writes=["nmT"])
                        if not first:
                            kb.plan("v", lambda v, col=col: v.tensor_tensor(
                                out=nmT[0:1, col:col + 1], in0=nmT[0:1, col:col + 1],
                                in1=nmT[0:1, 20:21], op=ALU.max),
                                reads=["nmT"], writes=["nmT"])
                    sqs.append(t_sq)
                    nms.append(t_norm)
                if nsub >= 2:
                    for i in range(0, nsub, 2):
                        th.append(sqs[i])
                        th.append(sqs[i + 1])
                        if i >= 2:
                            th.append(nms[i - 2])
                            th.append(nms[i - 1])
                    th.append(nms[nsub - 2])
                    th.append(nms[nsub - 1])
                else:
                    th.extend(sqs)
                    th.extend(nms)

            def t_chain0(st, nm0=nm0, par=par):
                kb.plan("v", lambda v: v.tensor_tensor(
                    out=nmT[0:1, 16:17], in0=nmT[0:1, 0:1], in1=nmT[0:1, 1:2],
                    op=ALU.mult),
                    reads=["nmT"], writes=["nmT"])
                kb.plan("s", lambda s: s.activation(
                    out=nmT[0:1, 17:18], in_=nmT[0:1, 16:17], func=AF.Sqrt),
                    reads=["nmT"], writes=["nmT"])
                kb.plan("v", lambda v: v.tensor_scalar(
                    out=nmT[0:1, 16:17], in0=nmT[0:1, 17:18], scalar1=-1.0,
                    scalar2=None, op0=ALU.mult),
                    reads=["nmT"], writes=["nmT"])

            def t_chain(st, nm0=nm0, par=par):
                chain_bk = 2 * st
                kb.plan("t", lambda t: t.matmul(
                    psum[:, chain_bk * 512: chain_bk * 512 + 1],
                    ones_f[0:1, 0:128], nmT[0:1, 16:17],
                    start=True, stop=True, skip_group_check=True),
                    reads=["nmT", "ones_f"], writes=[f"pb{chain_bk}"])
                kb.plan("v", lambda v: v.tensor_copy(
                    out=negM[:, nm0:nm0 + 1],
                    in_=psum[:, chain_bk * 512: chain_bk * 512 + 1]),
                    reads=[f"pb{chain_bk}"], writes=[f"negM{par}"])
            th.append(t_chain0)
            th.append(t_chain)
            return th

        th0 = qkv_thunks(0)
        _nps = S // 512

        def ln1_stats(s):
            c0 = s * 512
            bs, bq = (s % 2) * 2, (s % 2) * 2 + 1
            for i in range(DC):
                sl = i % 2
                kb.plan("s", lambda s, i=i, c0=c0, sl=sl: s.activation(
                    out=sqT[:, sl * 512:(sl + 1) * 512], in_=xbf[i][:, c0:c0 + 512],
                    func=AF.Square),
                    reads=[f"xbf{i}"], writes=[f"sqT{sl}"])
                kb.plan("t", lambda t, i=i, c0=c0, bs=bs: t.matmul(
                    PB(bs), ones_b[:, :], xbf[i][:, c0:c0 + 512],
                    start=(i == 0), stop=(i == DC - 1)),
                    reads=[f"xbf{i}", "ones_b"], writes=[f"pb{bs}"])
                kb.plan("t", lambda t, i=i, sl=sl, bq=bq: t.matmul(
                    PB(bq), ones_b[:, :], sqT[:, sl * 512:(sl + 1) * 512],
                    start=(i == 0), stop=(i == DC - 1)),
                    reads=[f"sqT{sl}", "ones_b"], writes=[f"pb{bq}"])

        def ln1_tail(s):
            c0 = s * 512
            bs, bq = (s % 2) * 2, (s % 2) * 2 + 1
            ln_finalize(bs, bq, 512, 1.0 / D)
            for i in range(DC):
                eng = "v" if i % 2 == 0 else "g"
                lt = lt1 if i % 2 == 0 else lt1g
                ltn = "lt1" if i % 2 == 0 else "meanT"
                kb.plan(eng, lambda e, i=i, c0=c0, lt=lt: e.tensor_tensor(
                    out=lt[:], in0=xbf[i][:, c0:c0 + 512], in1=rbcT[:], op=ALU.mult),
                    reads=[f"xbf{i}", "rbcT"], writes=[ltn])
                kb.plan(eng, lambda e, i=i, c0=c0, lt=lt: e.tensor_tensor(
                    out=xbf[i][:, c0:c0 + 512], in0=lt[:], in1=mrbcT[:], op=ALU.subtract),
                    reads=[ltn, "mrbcT"], writes=[f"xbf{i}"])
            _s = s - 2
            if _s >= 0:
                for f in (th0[_s], th0[_nps + _s], th0[2 * _nps + _s]):
                    f(2)

        def warm(n):
            for _ in range(n):
                kb.plan("t", lambda t: t.matmul(
                    psum[:, 3072:3200], ones_f[:, :], ones_f[:, 0:128],
                    start=True, stop=True, skip_group_check=True),
                    reads=["ones_f"], writes=[])

        warm(24)
        ln1_stats(0)
        for s in range(_nps):
            if s + 1 < _nps:
                ln1_stats(s + 1)
            ln1_tail(s)
            warm(8)

        for _s in range(max(0, _nps - 2), _nps):
            for f in (th0[_s], th0[_nps + _s], th0[2 * _nps + _s]):
                f(2)
        for i, f in enumerate(th0[3 * _nps:]):
            f(i % 3)

        def dump_and_stop(bufs_list):
            for i, bb in enumerate(bufs_list):
                kb.dma("g", f"Ddmp{i % 2}", lambda g, bb=bb, i=i: g.dma_start(
                    out=outT[i * 128:(i + 1) * 128, :], in_=bb[:]),
                    reads=[])
            kb.barrier("g")

        if upto == "ln1":
            dump_and_stop(xbf)
            actx.close()
            blk = ctx.enter_context(nc.Block())
            blk.tensor(lambda e: [f(e) for f in kb.prog["t"]])
            blk.scalar(lambda e: [f(e) for f in kb.prog["s"]])
            blk.vector(lambda e: [f(e) for f in kb.prog["v"]])
            blk.gpsimd(lambda e: [f(e) for f in kb.prog["g"]])
            blk.sync(lambda e: [f(e) for f in kb.prog["y"]])
            return nc

        for p in range(NPx):
            par = p % 2
            qP, kP, vP = qPs[par], kPs[par], vPs[par]
            qn, kn, vn = f"qP{par}", f"kP{par}", f"vP{par}"
            nm0 = par * 2
            nxt = qkv_thunks(p + 1) if (p + 1 < NPx) else []
            nxt_i = [0]
            _nqg = NQG
            if upto.startswith("att") or upto.startswith("fin") or upto in ("finA", "finB"):
                _nqg = 2 if upto == "att2" else 1
            NB3 = CT
            pending = []

            def plan_scores(qg, c, ping, st):
                q0 = qg * QG
                banks = [f"pb{2 * st}", f"pb{2 * st + 1}"]

                def mksc(t, c=c, q0=q0, st=st, kP=kP, qP=qP):
                    m = None
                    for (hrow, boff, tpos) in [(0, 0, (0, 0)), (64, 1, (64, 0))]:
                        m = t.matmul(
                            psum[:, (2 * st + boff) * 512: (2 * st + boff + 1) * 512],
                            kP[hrow:hrow + 64, c * 128:(c + 1) * 128],
                            qP[hrow:hrow + 64, q0:q0 + QG],
                            start=True, stop=True, skip_group_check=True,
                            tile_position=tpos)
                    return m
                kb.plan("t", mksc, reads=[kn, qn], writes=banks)
                kb.plan("s", lambda s, ping=ping, st=st, nm0=nm0: s.activation(
                    out=pT[:, ping * 1024: ping * 1024 + 1024],
                    in_=psum[:, 2 * st * 512: (2 * st + 2) * 512],
                    func=AF.Exp, bias=negM[:, nm0:nm0 + 1]),
                    reads=banks + [f"negM{par}"],
                    writes=[f"pT{ping}"])

            def plan_pv(qg, c, ping):
                def mkpv(t, c=c, ping=ping, vP=vP):
                    pA = pT[:, ping * 1024: ping * 1024 + 512]
                    pB = pT[:, ping * 1024 + 512: ping * 1024 + 1024]
                    m = t.matmul(psum[0:65, 3072:3584],
                                 vP[:, c * 130: c * 130 + 65], pA,
                                 start=(c == 0), stop=(c == CT - 1),
                                 skip_group_check=True)
                    m = t.matmul(psum[0:65, 3584:4096],
                                 vP[:, c * 130 + 65: c * 130 + 130], pB,
                                 start=(c == 0), stop=(c == CT - 1),
                                 skip_group_check=True)
                    return m
                kb.plan("t", mkpv, reads=[f"pT{ping}", vn],
                        writes=["pb6", "pb7"])

            def plan_fin_stage(qg):
                kb.plan("s", lambda s: s.activation(
                    out=rbc2[64:65, 0:1024], in_=psum[64:65, 3072:4096],
                    func=AF.Ln),
                    reads=["pb6", "pb7"], writes=["rbc2"])
                kb.plan("s", lambda s: s.activation(
                    out=recT[64:65, 0:1024], in_=rbc2[64:65, 0:1024],
                    func=AF.Exp, scale=-1.0),
                    reads=["rbc2"], writes=["recT"])

            def plan_fin_rest(qg, p=None, bk=0):
                q0 = qg * QG
                pbk = psum[:, bk * 512: (bk + 1) * 512]
                pbk2 = psum[:, (bk + 1) * 512: (bk + 2) * 512]
                kb.plan("t", lambda t: t.matmul(
                    pbk, ones_b[64:65, 0:128], recT[64:65, 0:512],
                    start=True, stop=True, skip_group_check=True),
                    reads=["recT", "ones_b"], writes=[f"pb{bk}"])
                kb.plan("t", lambda t: t.matmul(
                    pbk2, ones_b[64:65, 0:128], recT[64:65, 512:1024],
                    start=True, stop=True, skip_group_check=True),
                    reads=["recT", "ones_b"], writes=[f"pb{bk + 1}"])
                kb.plan("v", lambda v: v.tensor_copy(
                    out=rbc2[:, 0:512], in_=pbk),
                    reads=[f"pb{bk}"], writes=["rbc2"])
                kb.plan("v", lambda v: v.tensor_copy(
                    out=rbc2[:, 512:1024], in_=pbk2),
                    reads=[f"pb{bk + 1}"], writes=["rbc2"])
            def plan_fin_tt(qg, p=None):
                q0 = qg * QG
                stp = qg % 2
                kb.plan("v", lambda v, stp=stp: v.tensor_tensor(
                    out=att_st[stp][0:64, 0:QG],
                    in0=psum[0:64, 3072:3584], in1=rbc2[0:64, 0:512], op=ALU.mult),
                    reads=["pb6", "rbc2"], writes=[f"ast{stp}"])
                kb.plan("v", lambda v, stp=stp: v.tensor_tensor(
                    out=att_st[stp][0:64, QG:2 * QG],
                    in0=psum[0:64, 3584:4096], in1=rbc2[0:64, 512:1024],
                    op=ALU.mult),
                    reads=["pb7", "rbc2"], writes=[f"ast{stp}"])
                kb.dma("g", f"Dg_ast{stp}", lambda g, p=p, q0=q0, stp=stp: g.dma_start(
                    out=attnS[p * 128:p * 128 + 64, q0:q0 + QG],
                    in_=att_st[stp][0:64, 0:QG]),
                    reads=[f"ast{stp}"])
                kb.dma("g", f"Dg_ast{stp}", lambda g, p=p, q0=q0, stp=stp: g.dma_start(
                    out=attnS[p * 128 + 64:(p + 1) * 128, q0:q0 + QG],
                    in_=att_st[stp][0:64, QG:2 * QG]),
                    reads=[f"ast{stp}"])

            queue = []
            qseq = [0]

            def enq(due, f):
                queue.append((due, qseq[0], f))
                qseq[0] += 1

            ib = 0
            for qg in range(_nqg):
                for b in range(NB3):
                    if b == 1:
                        for it in sorted([x for x in queue if x[0] <= ib]):
                            it[2]()
                        queue = [x for x in queue if x[0] > ib]
                    plan_scores(qg, b, ib % 6, ib % 3)
                    for it in sorted([x for x in queue if x[0] <= ib]):
                        it[2]()
                    queue = [x for x in queue if x[0] > ib]
                    if ib >= 6 and b < NB3 - 2 and nxt_i[0] < len(nxt):
                        nxt[nxt_i[0]]((ib + 2) % 3)
                        nxt_i[0] += 1
                    if ib % 8 == 4:
                        p0_drain(1)
                    dpv = {0: 5, 1: 4, NB3 - 1: 2}.get(b, 3)
                    enq(ib + dpv, lambda qg=qg, b=b, ping=ib % 6: plan_pv(qg, b, ping))
                    if b == NB3 - 1 and not upto.startswith("att"):
                        enq(ib + 2, lambda qg=qg: plan_fin_stage(qg))
                        enq(ib + 3, lambda qg=qg, p=p, bk=2 * ((ib + 5) % 3):
                            plan_fin_rest(qg, p, bk))
                        enq(ib + 4, lambda qg=qg, p=p: plan_fin_tt(qg, p))
                    ib += 1
            for it in sorted(queue):
                it[2]()
            queue = []
            si = 0
            while nxt_i[0] < len(nxt):
                nxt[nxt_i[0]](si % 3)
                si += 1
                nxt_i[0] += 1

        if upto != "full":
            dump_and_stop(xbf)
            actx.close()
            blk = ctx.enter_context(nc.Block())
            blk.tensor(lambda e: [f(e) for f in kb.prog["t"]])
            blk.scalar(lambda e: [f(e) for f in kb.prog["s"]])
            blk.vector(lambda e: [f(e) for f in kb.prog["v"]])
            blk.gpsimd(lambda e: [f(e) for f in kb.prog["g"]])
            blk.sync(lambda e: [f(e) for f in kb.prog["y"]])
            return nc
        p0_drain(10 ** 9)
        actx.close()

        bctx = contextlib.ExitStack()
        w3sl = [bctx.enter_context(nc.sbuf_tensor(f"w3sl{i}", [128, 2048], BF16))
                for i in range(4)]
        hB = [bctx.enter_context(nc.sbuf_tensor(f"h{i}", [128, GB], BF16)) for i in range(HC)]
        attg = [bctx.enter_context(nc.sbuf_tensor(f"attg{dc}", [128, GB], BF16))
                for dc in range(DC)]
        g1B = [bctx.enter_context(nc.sbuf_tensor(f"g1{i}", [128, GB], BF16)) for i in range(HC)]
        projb = [bctx.enter_context(nc.sbuf_tensor(f"projb{j}", [128, GB], BF16))
                 for j in range(DC)]
        ln2b = [bctx.enter_context(nc.sbuf_tensor(f"ln2b{j}", [128, GB], BF16))
                for j in range(DC)]
        xld = [bctx.enter_context(nc.sbuf_tensor(f"xld{i}", [128, GB], F32)) for i in range(2)]
        x2R = [[bctx.enter_context(nc.sbuf_tensor(f"x2R{pp}_{j}", [128, GB], F32))
                for j in range(DC)] for pp in range(2)]
        x2b = [bctx.enter_context(nc.sbuf_tensor(f"x2b{i}", [128, GB], BF16))
               for i in range(2)]
        sab = [bctx.enter_context(nc.sbuf_tensor(f"sab{i}", [128, GB], BF16)) for i in range(2)]
        outf = xld

        for stm in "tsvgy":
            kb.barrier(stm)

        def head_thunks(g_):
            gs = g_ * GB
            par = g_ % 2
            for dc in range(DC):
                kb.dma("y", f"Dy_ag{dc}", lambda y, dc=dc, gs=gs:
                       y.dma_start(out=attg[dc][:],
                                   in_=attnS[dc * 128:(dc + 1) * 128, gs:gs + GB]),
                       writes=[f"ag_{dc}"])
            th = []
            for j in range(DC):
                jp = j % 2

                def t_wo(j=j, jp=jp, par=par):
                    kb.dma("y", f"Dy_wosl{jp}", lambda y, j=j, jp=jp: y.dma_start(
                        out=wosl[jp][:], in_=woB[j]),
                        writes=[f"wosl{jp}"])

                    def mkwo(t, jp=jp):
                        m = None
                        for dc in range(DC):
                            m = t.matmul(PB(2 + jp), wosl[jp][:, dc * 128:(dc + 1) * 128],
                                         attg[dc][:],
                                         start=(dc == 0), stop=(dc == DC - 1))
                        return m
                    kb.plan("t", mkwo,
                            reads=[f"wosl{jp}"] + [f"ag_{dc}" for dc in range(DC)],
                            writes=[f"pb{2 + jp}"])
                    kb.plan("v", lambda v, j=j, jp=jp: v.tensor_scalar(
                        out=projb[j][:], in0=PB(2 + jp), scalar1=bo_sb[:, j:j + 1],
                        scalar2=None, op0=ALU.add),
                        reads=[f"pb{2 + jp}", "bo_sb"], writes=[f"projb{j}"])
                th.append(t_wo)
            for j in range(DC):
                xp = j % 2

                def t_stsq(j=j, xp=xp):
                    kb.plan("s", lambda s, j=j, xp=xp: s.activation(
                        out=sqT[:, xp * 512:(xp + 1) * 512], in_=projb[j][:],
                        func=AF.Square),
                        reads=[f"projb{j}"], writes=[f"sqT{xp}"])

                def t_st(j=j, xp=xp):
                    kb.plan("t", lambda t, j=j: t.matmul(
                        PB(4), ones_b[:, :], projb[j][:],
                        start=(j == 0), stop=(j == DC - 1), skip_group_check=True),
                        reads=[f"projb{j}", "ones_b"], writes=["pb4"])
                    kb.plan("t", lambda t, j=j, xp=xp: t.matmul(
                        PB(5), ones_b[:, :], sqT[:, xp * 512:(xp + 1) * 512],
                        start=(j == 0), stop=(j == DC - 1), skip_group_check=True),
                        reads=[f"sqT{xp}", "ones_b"], writes=["pb5"])
                th.append(t_stsq)
                th.append(t_st)
            th.append(lambda: ln_finalize(4, 5, GB, 1.0 / D))
            for j in range(DC):

                def t_x2a(j=j, par=par, gs=gs):
                    jp = j % 2
                    kb.dma("y", f"Dy_xld{jp}", lambda y, j=j, jp=jp, gs=gs: y.dma_start(
                        out=xld[jp][:], in_=xT[j * 128:(j + 1) * 128, gs:gs + GB]),
                        writes=[f"xld{jp}"])
                    eng = "v" if j % 2 == 0 else "g"
                    lt = lt1 if j % 2 == 0 else lt1g
                    ltn = "lt1" if j % 2 == 0 else "meanT"
                    kb.plan(eng, lambda e, j=j, lt=lt: e.tensor_tensor(
                        out=lt[:], in0=projb[j][:], in1=rbcT[:], op=ALU.mult),
                        reads=[f"projb{j}", "rbcT"], writes=[ltn])
                    kb.plan(eng, lambda e, lt=lt: e.tensor_tensor(
                        out=lt[:], in0=lt[:], in1=mrbcT[:], op=ALU.subtract),
                        reads=[ltn, "mrbcT"], writes=[ltn])
                    kb.plan(eng, lambda e, j=j, jp=jp, par=par, lt=lt: e.tensor_tensor(
                        out=x2R[par][j][:], in0=lt[:], in1=xld[jp][:], op=ALU.add),
                        reads=[ltn, f"xld{jp}"], writes=[f"x2R{par}_{j}"])
                    xp = j % 2
                    kb.plan("v", lambda v, j=j, xp=xp, par=par: v.tensor_copy(
                        out=x2b[xp][:], in_=x2R[par][j][:]),
                        reads=[f"x2R{par}_{j}"], writes=[f"x2b{xp}"])
                    kb.plan("s", lambda s, xp=xp, j=j: s.activation(
                        out=sqT[:, xp * 512:(xp + 1) * 512], in_=x2b[xp][:],
                        func=AF.Square),
                        reads=[f"x2b{xp}"], writes=[f"sqT{xp}"])

                def t_x2b(j=j):
                    xp = j % 2
                    kb.plan("t", lambda t, j=j, xp=xp: t.matmul(
                        PB(4), ones_b[:, :], x2b[xp][:],
                        start=(j == 0), stop=(j == DC - 1), skip_group_check=True),
                        reads=[f"x2b{xp}", "ones_b"], writes=["pb4"])
                    kb.plan("t", lambda t, j=j, xp=xp: t.matmul(
                        PB(5), ones_b[:, :], sqT[:, xp * 512:(xp + 1) * 512],
                        start=(j == 0), stop=(j == DC - 1), skip_group_check=True),
                        reads=[f"sqT{xp}", "ones_b"], writes=["pb5"])
                th.append(t_x2a)
                th.append(t_x2b)
            th.append(lambda: ln_finalize(4, 5, GB, 1.0 / D))
            for j in range(DC):

                def t_ln2(j=j, par=par):
                    kb.plan("v", lambda v, j=j, par=par: v.tensor_tensor(
                        out=lt1[:], in0=x2R[par][j][:], in1=rbcT[:], op=ALU.mult),
                        reads=[f"x2R{par}_{j}", "rbcT"], writes=["lt1"])
                    kb.plan("v", lambda v, j=j: v.tensor_tensor(
                        out=ln2b[j][:], in0=lt1[:], in1=mrbcT[:], op=ALU.subtract),
                        reads=["lt1", "mrbcT"], writes=[f"ln2b{j}"])
                th.append(t_ln2)
            return th

        for f in head_thunks(0):
            f()
        prev_out = []
        for g_ in range(NG):
            gs = g_ * GB
            par = g_ % 2
            nxt_h = head_thunks(g_ + 1) if g_ + 1 < NG else []
            hi_ = [0]

            def hdrain(nxt_h=nxt_h, hi_=hi_):
                if hi_[0] < len(nxt_h):
                    nxt_h[hi_[0]]()
                    hi_[0] += 1

            def w12_load(jj):
                jp = jj % 3
                kb.dma("y", f"Dy_w1sl{jp}", lambda y, jj=jj, jp=jp: y.dma_start(
                    out=w1sl[jp][:], in_=w1B[jj]),
                    writes=[f"w1sl{jp}"])
                kb.dma("y", f"Dy_w2sl{jp}", lambda y, jj=jj, jp=jp: y.dma_start(
                    out=w2sl[jp][:], in_=w2B[jj]),
                    writes=[f"w2sl{jp}"])
            w12_load(0)
            w12_load(1)
            po_i = [0]
            for jj in range(HC):
                jp = jj % 3
                bka, bkb = (jj % 2) * 2, (jj % 2) * 2 + 1
                if jj + 2 < HC:
                    w12_load(jj + 2)
                if jj % 4 == 1 and po_i[0] < len(prev_out):
                    prev_out[po_i[0]]()
                    po_i[0] += 1

                def mkw1(t, jp=jp, bka=bka):
                    m = None
                    for i in range(DC):
                        m = t.matmul(PB(bka), w1sl[jp][:, i * 128:(i + 1) * 128], ln2b[i][:],
                                     start=(i == 0), stop=(i == DC - 1))
                    return m
                kb.plan("t", mkw1, reads=[f"w1sl{jp}"] + [f"ln2b{i}" for i in range(DC)],
                        writes=[f"pb{bka}"])

                def mkw2(t, jp=jp, bkb=bkb):
                    m = None
                    for i in range(DC):
                        m = t.matmul(PB(bkb), w2sl[jp][:, i * 128:(i + 1) * 128], ln2b[i][:],
                                     start=(i == 0), stop=(i == DC - 1))
                    return m
                kb.plan("t", mkw2, reads=[f"w2sl{jp}"] + [f"ln2b{i}" for i in range(DC)],
                        writes=[f"pb{bkb}"])
                sp = jj % 2
                kb.plan("s", lambda s, jj=jj, sp=sp, bka=bka: s.activation(
                    out=sab[sp][:], in_=PB(bka), func=AF.Silu, scale=w1s_sb[:, jj:jj + 1]),
                    reads=[f"pb{bka}", "w1s_sb"], writes=[f"sab{sp}"])
                kb.plan("v", lambda v, jj=jj, sp=sp, bkb=bkb: v.scalar_tensor_tensor(
                    out=hB[jj][:], in0=PB(bkb), scalar=w2s_sb[:, jj:jj + 1],
                    in1=sab[sp][:], op0=ALU.mult, op1=ALU.mult),
                    reads=[f"pb{bkb}", "w2s_sb", f"sab{sp}"], writes=[f"h{jj}"])

            def h1_load(jj):
                jp = jj % 2
                for hh in range(2):
                    kb.dma("y", f"Dy_h1sl{jp * 2 + hh}", lambda y, jj=jj, jp=jp, hh=hh:
                           y.dma_start(out=h1sl[jp * 2 + hh][:],
                                       in_=h1B[jj, :, hh * 2048:(hh + 1) * 2048]),
                           writes=[f"h1sl{jp * 2 + hh}"])
            h1_load(0)
            for jj in range(HC):
                jp = jj % 2
                bk = 4 + jp
                if jj + 1 < HC:
                    h1_load(jj + 1)

                def mkh1(t, jp=jp, bk=bk):
                    m = None
                    for i in range(HC):
                        sl = h1sl[jp * 2 + (i // 16)]
                        m = t.matmul(PB(bk), sl[:, (i % 16) * 128:(i % 16 + 1) * 128], hB[i][:],
                                     start=(i == 0), stop=(i == HC - 1))
                    return m
                kb.plan("t", mkh1, reads=[f"h1sl{jp * 2}", f"h1sl{jp * 2 + 1}"]
                        + [f"h{i}" for i in range(HC)], writes=[f"pb{bk}"])
                kb.plan("s", lambda s, jj=jj, bk=bk: s.activation(
                    out=g1B[jj][:], in_=PB(bk), func=AF.Gelu, scale=h1s_sb[:, jj:jj + 1]),
                    reads=[f"pb{bk}", "h1s_sb"], writes=[f"g1{jj}"])

            def h2_load(jj):
                jp = jj % 2
                for hh in range(2):
                    kb.dma("g", f"Dg_h2sl{jp * 2 + hh}", lambda g, jj=jj, jp=jp, hh=hh:
                           g.dma_start(out=h2sl[jp * 2 + hh][:],
                                       in_=h2B[jj, :, hh * 2048:(hh + 1) * 2048]),
                           writes=[f"h2sl{jp * 2 + hh}"])
            h2_load(0)
            for jj in range(HC):
                jp = jj % 2
                bk = 6 + jp
                if jj + 1 < HC:
                    h2_load(jj + 1)

                def mkh2(t, jp=jp, bk=bk):
                    m = None
                    for i in range(HC):
                        sl = h2sl[jp * 2 + (i // 16)]
                        m = t.matmul(PB(bk), sl[:, (i % 16) * 128:(i % 16 + 1) * 128], g1B[i][:],
                                     start=(i == 0), stop=(i == HC - 1))
                    return m
                kb.plan("t", mkh2, reads=[f"h2sl{jp * 2}", f"h2sl{jp * 2 + 1}"]
                        + [f"g1{i}" for i in range(HC)], writes=[f"pb{bk}"])
                kb.plan("s", lambda s, jj=jj, bk=bk: s.activation(
                    out=hB[jj][:], in_=PB(bk), func=AF.Gelu, scale=h2s_sb[:, jj:jj + 1]),
                    reads=[f"pb{bk}", "h2s_sb"], writes=[f"h{jj}"])
                hdrain()
                if jj >= 16:
                    hdrain()

            def w3_load(j):
                jp = j % 2
                for hh in range(2):
                    kb.dma("g", f"Dg_w3sl{jp * 2 + hh}", lambda g, j=j, jp=jp, hh=hh:
                           g.dma_start(out=w3sl[jp * 2 + hh][:],
                                       in_=w3B[j, :, hh * 2048:(hh + 1) * 2048]),
                           writes=[f"w3sl{jp * 2 + hh}"])
            w3_load(0)
            for j in range(DC):
                hdrain()
                jp = j % 2
                bk = 2 + jp
                if j + 1 < DC:
                    w3_load(j + 1)

                def mkw3(t, jp=jp, bk=bk):
                    m = None
                    for i in range(HC):
                        sl = w3sl[jp * 2 + (i // 16)]
                        m = t.matmul(PB(bk), sl[:, (i % 16) * 128:(i % 16 + 1) * 128], hB[i][:],
                                     start=(i == 0), stop=(i == HC - 1))
                    return m
                kb.plan("t", mkw3, reads=[f"w3sl{jp * 2}", f"w3sl{jp * 2 + 1}"]
                        + [f"h{i}" for i in range(HC)], writes=[f"pb{bk}"])
                kb.plan("v", lambda v, j=j, bk=bk: v.tensor_scalar(
                    out=projb[j][:], in0=PB(bk), scalar1=w3s_sb[:, j:j + 1],
                    scalar2=None, op0=ALU.mult),
                    reads=[f"pb{bk}", "w3s_sb"], writes=[f"projb{j}"])
            while hi_[0] < len(nxt_h):
                nxt_h[hi_[0]]()
                hi_[0] += 1
            for j in range(DC):
                kb.plan("s", lambda s, j=j: s.activation(
                    out=hB[j][:], in_=projb[j][:], func=AF.Square),
                    reads=[f"projb{j}"], writes=[f"h{j}"])
            for j in range(DC):
                kb.plan("t", lambda t, j=j: t.matmul(
                    PB(2), ones_b[:, :], projb[j][:], start=(j == 0), stop=(j == DC - 1)),
                    reads=[f"projb{j}", "ones_b"], writes=["pb2"])
                kb.plan("t", lambda t, j=j: t.matmul(
                    PB(3), ones_b[:, :], hB[j][:], start=(j == 0), stop=(j == DC - 1)),
                    reads=[f"h{j}", "ones_b"], writes=["pb3"])
            ln_finalize(2, 3, GB, 1.0 / D)

            def t_out(j, par=par, gs=gs):
                jp = j % 2
                kb.plan("v", lambda v, j=j: v.tensor_tensor(
                    out=lt1[:], in0=projb[j][:], in1=rbcT[:], op=ALU.mult),
                    reads=[f"projb{j}", "rbcT"], writes=["lt1"])
                kb.plan("v", lambda v: v.tensor_tensor(
                    out=lt1[:], in0=lt1[:], in1=mrbcT[:], op=ALU.subtract),
                    reads=["lt1", "mrbcT"], writes=["lt1"])
                kb.plan("v", lambda v, j=j, jp=jp, par=par: v.tensor_tensor(
                    out=outf[jp][:], in0=lt1[:], in1=x2R[par][j][:], op=ALU.add),
                    reads=["lt1", f"x2R{par}_{j}"], writes=[f"xld{jp}"])
                kb.dma("g", f"Dg_out{jp}", lambda g, j=j, jp=jp, gs=gs: g.dma_start(
                    out=outT[j * 128:(j + 1) * 128, gs:gs + GB], in_=outf[jp][:]),
                    reads=[f"xld{jp}"])
            if g_ + 1 < NG:
                prev_out = [lambda j=j, t_out=t_out: t_out(j) for j in range(DC)]
            else:
                for j in range(DC):
                    t_out(j)

        bctx.close()
        kb.barrier("g", ["Dg_out0", "Dg_out1"])

        block = ctx.enter_context(nc.Block())
        block.tensor(lambda e: [f(e) for f in kb.prog["t"]])
        block.scalar(lambda e: [f(e) for f in kb.prog["s"]])
        block.vector(lambda e: [f(e) for f in kb.prog["v"]])
        block.gpsimd(lambda e: [f(e) for f in kb.prog["g"]])
        block.sync(lambda e: [f(e) for f in kb.prog["y"]])

    return nc


def kernel(L0_attn_x, L0_attn_Wq, L0_attn_Wk, L0_attn_Wv, L0_attn_Wo, L0_attn_bo,
           L0_ffn_w1_w, L0_ffn_w1_s, L0_ffn_w2_w, L0_ffn_w2_s,
           L0_ffn_h1_w, L0_ffn_h1_s, L0_ffn_h2_w, L0_ffn_h2_s,
           L0_ffn_w3_w, L0_ffn_w3_s):
    x = np.asarray(L0_attn_x)
    B, S, _ = x.shape
    assert B == NCORES and x.shape[2] == D

    cT = lambda a: np.ascontiguousarray(np.asarray(a, dtype=np.float32).T)
    shared = {
        "wqT": cT(L0_attn_Wq), "wkT": cT(L0_attn_Wk),
        "wvT": cT(L0_attn_Wv), "woT": cT(L0_attn_Wo),
        "bo_r": np.ascontiguousarray(np.asarray(L0_attn_bo, np.float32).reshape(DC, 128).T),
        "w1T": cT(L0_ffn_w1_w), "w2T": cT(L0_ffn_w2_w), "w3T": cT(L0_ffn_w3_w),
        "h1T": cT(L0_ffn_h1_w), "h2T": cT(L0_ffn_h2_w),
        "w1s": np.ascontiguousarray(np.asarray(L0_ffn_w1_s, np.float32).reshape(HC, 128).T),
        "w2s": np.ascontiguousarray(np.asarray(L0_ffn_w2_s, np.float32).reshape(HC, 128).T),
        "h1s": np.ascontiguousarray(np.asarray(L0_ffn_h1_s, np.float32).reshape(HC, 128).T),
        "h2s": np.ascontiguousarray(np.asarray(L0_ffn_h2_s, np.float32).reshape(HC, 128).T),
        "w3s": np.ascontiguousarray(np.asarray(L0_ffn_w3_s, np.float32).reshape(DC, 128).T),
    }
    in_maps = []
    for c in range(NCORES):
        m = dict(shared)
        m["xT"] = np.ascontiguousarray(x[c].T.astype(np.float32))
        in_maps.append(m)

    nc = build(S)
    res = run_bass_kernel_spmd(nc, in_maps, core_ids=list(range(NCORES)))
    out = np.stack([np.asarray(r["outT"], dtype=np.float32).T for r in res.results], axis=0)
    return out
